# version 05
# baseline (speedup 1.0000x reference)
import numpy as np
import ml_dtypes
from contextlib import ExitStack

import concourse.bass as bass
import concourse.mybir as mybir
from concourse.bass_utils import run_bass_kernel_spmd

F32 = mybir.dt.float32
I32 = mybir.dt.int32
BF16 = mybir.dt.bfloat16
ALU = mybir.AluOpType
ACT = mybir.ActivationFunctionType
AX = mybir.AxisListType

NCORE = 8
D = 1024
H = 16
E = 64
S = 2048
LCORE = 4096
LTILE = 512
NSUB = LTILE // 128
NT = LCORE // LTILE
NCHUNK = S // 128
NIC = D // 128
ATT_SCALE = 0.125

ENGINES = ("sync", "gpsimd", "tensor", "vector", "scalar")


class Ctr:
    __slots__ = ("sem", "n", "name")

    def __init__(self, sem, name):
        self.sem = sem
        self.n = 0
        self.name = name


class Plan:
    def __init__(self, nc):
        self.nc = nc
        self.ops = {e: [] for e in ENGINES}
        self.trace = {e: [] for e in ENGINES}
        self.hs_ctr = None

    def emit(self, eng, fn, inc=None, k=1, hs=False):
        if eng == "vector" and self.hs_ctr is not None and inc is None:
            inc = self.hs_ctr
        hs = hs and eng == "vector"
        tgt = None
        if inc is not None:
            inc.n += k
            tgt = inc.n

            def wrapped(e, fn=fn, inc=inc, k=k):
                fn(e).then_inc(inc.sem, k)

            self.ops[eng].append(wrapped)
            self.trace[eng].append(("inc", inc.name, k))
        else:
            self.ops[eng].append(fn)
            self.trace[eng].append(("op",))
        if hs:
            self.wait(eng, inc, tgt)
        return tgt

    def wait(self, eng, ctr, val):
        if val is None or val <= 0:
            return

        def fn(e, sem=ctr.sem, val=val):
            e.wait_ge(sem, val)

        self.ops[eng].append(fn)
        self.trace[eng].append(("wait", ctr.name, val))

    def check_deadlock(self):
        pos = {e: 0 for e in ENGINES}
        sems = {}
        progressed = True
        while progressed:
            progressed = False
            for e in ENGINES:
                tr = self.trace[e]
                while pos[e] < len(tr):
                    it = tr[pos[e]]
                    if it[0] == "wait" and sems.get(it[1], 0) < it[2]:
                        break
                    if it[0] == "inc":
                        sems[it[1]] = sems.get(it[1], 0) + it[2]
                    pos[e] += 1
                    progressed = True
        stuck = {e: (pos[e], len(self.trace[e]), self.trace[e][pos[e]])
                 for e in ENGINES if pos[e] < len(self.trace[e])}
        if stuck:
            raise RuntimeError(f"deadlock in plan: {stuck}; sems={sems}")


def build_nc(stage=3):
    nc = bass.Bass("TRN2", target_bir_lowering=False)

    xin = nc.dram_tensor("xin", [LCORE, D], F32, kind="ExternalInput")
    src_h = nc.dram_tensor("src", [S, D], F32, kind="ExternalInput")
    val_h = nc.dram_tensor("val", [S, D], F32, kind="ExternalInput")
    wq_h = nc.dram_tensor("wq", [D, D], F32, kind="ExternalInput")
    wk_h = nc.dram_tensor("wk", [D, D], F32, kind="ExternalInput")
    wv_h = nc.dram_tensor("wv", [D, D], F32, kind="ExternalInput")
    wo_h = nc.dram_tensor("wo", [D, D], F32, kind="ExternalInput")
    wd_h = nc.dram_tensor("wd", [D, D], F32, kind="ExternalInput")
    bq_h = nc.dram_tensor("bq", [1, D], F32, kind="ExternalInput")
    bk_h = nc.dram_tensor("bk", [1, D], F32, kind="ExternalInput")
    bv_h = nc.dram_tensor("bv", [1, D], F32, kind="ExternalInput")
    bo_h = nc.dram_tensor("bo", [1, D], F32, kind="ExternalInput")
    fb_h = nc.dram_tensor("fb", [1, D], F32, kind="ExternalInput")
    g_h = nc.dram_tensor("g_rep", [128, D], F32, kind="ExternalInput")
    b_h = nc.dram_tensor("b_rep", [128, D], F32, kind="ExternalInput")
    id_h = nc.dram_tensor("ident", [128, 128], BF16, kind="ExternalInput")
    yout = nc.dram_tensor("yout", [LCORE, D], F32, kind="ExternalOutput")
    dbg = stage == 1
    if dbg:
        dbg_ln = nc.dram_tensor("dbg_ln", [128, D], F32, kind="ExternalOutput").ap()
        dbg_xd = nc.dram_tensor("dbg_xd", [128, D], BF16, kind="ExternalOutput").ap()
        dbg_xdT = nc.dram_tensor("dbg_xdT", [128, NIC, 128], BF16, kind="ExternalOutput").ap()
        dbg_wdT = nc.dram_tensor("dbg_wdT", [128, NIC, D], BF16, kind="ExternalOutput").ap()
    dbg2 = stage == 2
    if dbg2:
        dbg_qn = nc.dram_tensor("dbg_qn", [128, D], BF16, kind="ExternalOutput").ap()
        dbg_knT = nc.dram_tensor("dbg_knT", [128, 128], BF16, kind="ExternalOutput").ap()
        dbg_pt = nc.dram_tensor("dbg_pt", [128, D], BF16, kind="ExternalOutput").ap()
        dbg_ctx = nc.dram_tensor("dbg_ctx", [64, D], F32, kind="ExternalOutput").ap()
        dbg_rs = nc.dram_tensor("dbg_rs", [1, D], F32, kind="ExternalOutput").ap()
        dbg_vp = nc.dram_tensor("dbg_vp", [128, 130], BF16, kind="ExternalOutput").ap()

    def xin_tile_ap(lt):
        return bass.AP(xin, lt * LTILE * D, [[D, 128], [128 * D, NSUB], [1, D]])

    def yout_tile_ap(lt):
        return bass.AP(yout, lt * LTILE * D, [[D, 128], [128 * D, NSUB], [1, D]])

    def row128_ap(h, i):
        return bass.AP(h, 128 * i * D, [[D, 128], [1, D]])

    es = ExitStack()
    sb = lambda nm, shp, dt=F32: es.enter_context(nc.sbuf_tensor(nm, shp, dt))
    mkpsum = lambda nm, shp, dt=F32: es.enter_context(nc.psum_tensor(nm, shp, dt))

    ident = sb("ident_t", [128, 128], BF16)
    wq = sb("wq_t", [128, NIC, D], BF16)
    wo = sb("wo_t", [128, NIC, D], BF16)
    wdT = sb("wdT_t", [128, NIC, D], BF16)
    knT = sb("knT_t", [128, NIC, S], BF16)
    vplus = sb("vplus_t", [128, NCHUNK, H * 65], BF16)
    g_rep = sb("g_rep_t", [128, D], F32)
    b_rep = sb("b_rep_t", [128, D], F32)
    ones_row = sb("ones_row", [1, 128], BF16)
    bq_r = sb("bq_r", [1, D], BF16)
    bk_r = sb("bk_r", [1, D], BF16)
    bv_r = sb("bv_r", [1, D], BF16)
    bo_r = sb("bo_r", [1, D], BF16)
    fb_r = sb("fb_r", [1, D], BF16)
    TL = sb("TL", [128, D], F32)
    T1 = sb("T1", [128, D], F32)
    T2 = sb("T2", [128, D], F32)
    I1 = sb("I1", [128, D], I32)
    I2 = sb("I2", [128, D], I32)
    AMX = sb("AMX", [128, 32], F32)
    SCL = sb("SCL", [128, 32], F32)
    RS8 = sb("RS8", [128, 32], F32)
    SC8 = sb("SC8", [128, 32], F32)
    stat6 = sb("stat6", [128, 12], F32)
    mv = sb("mv", [128, 2], F32)
    sd1 = sb("sd1", [128, 1], F32)
    rstd = sb("rstd", [128, 1], F32)
    nq16 = sb("nq16", [128, H], F32)
    rq16 = sb("rq16", [128, H], F32)
    xd_bf = sb("xd_bf", [128, D], BF16)
    ln_bf = sb("ln_bf", [128, D], BF16)
    qn_bf = sb("qn_bf", [128, D], BF16)
    stage_bf = ln_bf[64:128, 0:512]

    psA = mkpsum("psA", [128, 1024], F32)
    psB = mkpsum("psB", [128, 1024], F32)
    ctxps = mkpsum("ctxps", [128, 1024], F32)
    trpsA = mkpsum("trpsA", [128, 1024], BF16)
    trpsB = mkpsum("trpsB", [128, 1024], BF16)
    psname = {id(psA): "psA", id(psB): "psB"}

    p = Plan(nc)
    LD = Ctr(es.enter_context(nc.semaphore("LD")), "LD")
    ST = Ctr(es.enter_context(nc.semaphore("ST")), "ST")
    TS = Ctr(es.enter_context(nc.semaphore("TS")), "TS")
    VS = Ctr(es.enter_context(nc.semaphore("VS")), "VS")
    SC = Ctr(es.enter_context(nc.semaphore("SC")), "SC")
    GP = Ctr(es.enter_context(nc.semaphore("GP")), "GP")
    p.hs_ctr = VS

    free = {}

    def wait_free(eng, key):
        if key in free and free[key] is not None:
            ctr, val = free[key]
            p.wait(eng, ctr, val)

    def dma_in(out_ap, in_ap):
        return p.emit("sync", lambda e, o=out_ap, i=in_ap:
                      e.dma_start(out=o, in_=i), inc=LD, k=16)

    def bcast0(t, nalloc, n, inner):
        return bass.AP(t, 0, [[nalloc, 128], [1, n], [0, inner]])

    tr_par = [0]

    def trans_seq(src_fn, dst3_fn, n, after=None):
        par = tr_par[0]
        tr_par[0] ^= 1
        trp = trpsA if par == 0 else trpsB
        if after is not None:
            if isinstance(after, tuple):
                p.wait("tensor", after[0], after[1])
            else:
                p.wait("tensor", VS, after)
        wait_free("tensor", ("trps", par))
        t_tgt = None
        for j in range(n):
            t_tgt = p.emit("tensor", lambda e, s=src_fn(j), o=trp, j=j:
                           e.transpose(o[:, 128 * j:128 * j + 128], s,
                                       ident[:]), inc=TS)
        p.wait("scalar", TS, t_tgt)
        s_tgt = p.emit("scalar", lambda e, d=dst3_fn(), t=trp, n=n: e.copy(
            d, t[:, 0:128 * n].rearrange("p (a b) -> p a b", b=128)), inc=SC)
        free[("trps", par)] = (SC, s_tgt)
        return s_tgt

    def layernorm(src_ap, eps, out_ap):
        p.emit("vector", lambda e, s=src_ap: e.bn_stats(stat6[:, 0:6], s[:, 0:512]), hs=True)
        p.emit("vector", lambda e, s=src_ap: e.bn_stats(stat6[:, 6:12], s[:, 512:1024]), hs=True)
        p.emit("vector", lambda e: e.bn_aggr(mv[:, :], stat6[:, :]), hs=True)
        v1 = p.emit("vector", lambda e, ep=eps: e.tensor_scalar(
            sd1[:, :], mv[:, 1:2], ep, None, ALU.add), inc=VS)
        p.wait("scalar", VS, v1)
        p.emit("scalar", lambda e: e.activation(rstd[:, :], sd1[:, :], ACT.Ln))
        s1 = p.emit("scalar", lambda e: e.activation(
            sd1[:, :], rstd[:, :], ACT.Exp, bias=0.0, scale=-0.5), inc=SC)
        p.wait("vector", SC, s1)
        return p.emit("vector", lambda e, s=src_ap, o=out_ap: e.tensor_scalar(
            o, s, mv[:, 0:1], sd1[:, :], ALU.subtract, ALU.mult), inc=VS, hs=True)

    def quantize(src_t, out_bf_ap3):
        src3 = src_t[:].rearrange("p (a b) -> p a b", b=32)
        wait_free("vector", "qtemps")
        p.emit("vector", lambda e, s=src3: e.tensor_reduce(
            AMX[:, :], s, AX.X, ALU.max, apply_absolute_value=True), hs=True)
        p.emit("vector", lambda e: e.tensor_scalar(
            SCL[:, :], AMX[:, :], 1.0 / 7.5, 1e-30, ALU.mult, ALU.max), hs=True)
        p.emit("vector", lambda e: e.reciprocal(AMX[:, :], SCL[:, :]), hs=True)
        p.emit("vector", lambda e: e.tensor_scalar(
            RS8[:, :], AMX[:, :], 8.0, None, ALU.mult), hs=True)
        p.emit("vector", lambda e: e.tensor_scalar(
            SC8[:, :], SCL[:, :], 0.125, None, ALU.mult), hs=True)
        t1_3 = T1[:].rearrange("p (a b) -> p a b", b=32)
        p.emit("vector", lambda e, s=src3: e.tensor_tensor(
            t1_3, bcast0(RS8, 32, 32, 32), s, ALU.mult))
        p.emit("vector", lambda e: e.tensor_scalar(
            I1[:, :], T1[:, :], 16.0, -16.0, ALU.min, ALU.max), hs=True)
        p.emit("vector", lambda e: e.tensor_scalar(
            T2[:, :], T1[:, :], 0.5, None, ALU.mult), hs=True)
        p.emit("vector", lambda e: e.tensor_scalar(
            T1[:, :], T2[:, :], 16.0, -16.0, ALU.min, ALU.max), hs=True)
        p.emit("vector", lambda e: e.tensor_scalar(
            TL[:, :], T2[:, :], 8.0, -8.0, ALU.min, ALU.max), hs=True)
        p.emit("vector", lambda e: e.tensor_tensor(
            I2[:, :], T1[:, :], TL[:, :], ALU.subtract))
        p.emit("vector", lambda e: e.tensor_scalar(
            T1[:, :], T2[:, :], 0.5, None, ALU.mult), hs=True)
        p.emit("vector", lambda e: e.tensor_scalar(
            TL[:, :], T1[:, :], 8.0, -8.0, ALU.min, ALU.max), hs=True)
        p.emit("vector", lambda e: e.scalar_tensor_tensor(
            T2[:, :], I2[:, :], 2.0, I1[:, :], ALU.mult, ALU.add), hs=True)
        p.emit("vector", lambda e: e.tensor_tensor(
            I1[:, :], T1[:, :], TL[:, :], ALU.subtract))
        p.emit("vector", lambda e: e.scalar_tensor_tensor(
            T1[:, :], I1[:, :], 4.0, T2[:, :], ALU.mult, ALU.add), hs=True)
        s2v = VS.n
        p.wait("gpsimd", VS, s2v)
        wait_free("gpsimd", "xdbuf")
        tgt = p.emit("gpsimd", lambda e, o=out_bf_ap3: e.tensor_tensor(
            o, bcast0(SC8, 32, 32, 32), t1_3, ALU.mult), inc=GP)
        free["qtemps"] = (GP, tgt)
        return tgt

    def proj_matmuls(ps, lhsT_fn, rhs_t, bias_r, operands_ready):
        wait_free("tensor", psname[id(ps)])
        if isinstance(operands_ready, tuple):
            p.wait("tensor", operands_ready[0], operands_ready[1])
        else:
            p.wait("tensor", VS, operands_ready)
        tgt = None
        for half in range(2):
            lo = 512 * half
            for j in range(NIC):
                p.emit("tensor", lambda e, ps=ps, lo=lo, l=lhsT_fn(j),
                       r=rhs_t[:, j, lo:lo + 512], st=(j == 0):
                       e.matmul(ps[:, lo:lo + 512], l, r, start=st, stop=False))
            tgt = p.emit("tensor", lambda e, ps=ps, lo=lo, b=bias_r:
                         e.matmul(ps[:, lo:lo + 512], ones_row[0:1, :],
                                  b[:, lo:lo + 512], start=False, stop=True),
                         inc=TS)
        return tgt

    id_ld = dma_in(ident[:], id_h.ap())
    dma_in(g_rep[:], g_h.ap())
    dma_in(b_rep[:], b_h.ap())
    p.wait("tensor", LD, id_ld)

    for hsrc, dstr in [(bq_h, bq_r), (bk_h, bk_r), (bv_h, bv_r),
                       (bo_h, bo_r), (fb_h, fb_r)]:
        p.emit("gpsimd", lambda e, i=hsrc.ap(), o=dstr[:, :]:
               e.dma_start(out=o, in_=i), inc=ST, k=16)
    bias_ld = ST.n
    p.wait("tensor", ST, bias_ld)

    p.emit("vector", lambda e: e.memset(ones_row[:, :], 1.0))
    ones_v = p.emit("vector", lambda e: e.memset(
        bass.AP(vplus, 64, [[NCHUNK * H * 65, 128], [H * 65, NCHUNK], [65, H]]),
        1.0), inc=VS)

    pre = ExitStack()
    wk = pre.enter_context(nc.sbuf_tensor("wk_t", [128, NIC, D], BF16))
    wv = pre.enter_context(nc.sbuf_tensor("wv_t", [128, NIC, D], BF16))
    wdq_c = pre.enter_context(nc.sbuf_tensor("wdq_c", [128, D], BF16))
    stgA = pre.enter_context(nc.sbuf_tensor("stgA", [128, D], F32))
    stgB = pre.enter_context(nc.sbuf_tensor("stgB", [128, D], F32))
    srcT = ln_bf[:].rearrange("p (a b) -> p a b", b=128)
    src_bf = qn_bf
    kn_bf = xd_bf
    valT = kn_bf[:].rearrange("p (a b) -> p a b", b=128)
    valT2 = valT

    stg_free = {"stgA": 0, "stgB": 0}
    for wi, (h, dst) in enumerate([(wq_h, wq), (wk_h, wk), (wv_h, wv),
                                   (wo_h, wo)]):
        for i in range(NIC):
            key = "stgA" if (wi * NIC + i) % 2 == 0 else "stgB"
            stg = stgA if key == "stgA" else stgB
            sf = stg_free[key]
            if isinstance(sf, tuple):
                p.wait("sync", GP, sf[1])
            else:
                p.wait("sync", VS, sf)
            ld = dma_in(stg[:, :], row128_ap(h, i))
            p.wait("vector", LD, ld)
            stg_free[key] = p.emit(
                "vector", lambda e, s=stg, d=dst[:, i, :]:
                e.tensor_copy(d, s[:, :]), inc=VS)
    for oc in range(NIC):
        key = "stgA" if oc % 2 == 0 else "stgB"
        stg = stgA if key == "stgA" else stgB
        sf = stg_free[key]
        if isinstance(sf, tuple):
            p.wait("sync", GP, sf[1])
        else:
            p.wait("sync", VS, sf)
        ld = dma_in(stg[:, :], row128_ap(wd_h, oc))
        p.wait("vector", LD, ld)
        qv = quantize(stg, wdq_c[:].rearrange("p (a b) -> p a b", b=32))
        stg_free[key] = ("GP", qv)
        trans_seq(lambda j: wdq_c[:, 128 * j:128 * j + 128],
                  lambda oc=oc: bass.AP(wdT, 128 * oc,
                                        [[NIC * D, 128], [D, NIC], [1, 128]]),
                  NIC, after=(GP, qv))
        free["xdbuf"] = (TS, TS.n)

    def elu_norm(src_ps, out_bf):
        p.emit("vector", lambda e, ps=src_ps: e.tensor_scalar(
            T1[:, :], ps[:, 0:1024], 0.0, None, ALU.min), hs=True)
        e2 = p.emit("vector", lambda e, ps=src_ps: e.tensor_scalar(
            T2[:, :], ps[:, 0:1024], 0.0, None, ALU.max), inc=VS, hs=True)
        p.wait("scalar", VS, e2)
        se = p.emit("scalar", lambda e: e.activation(TL[:, :], T1[:, :],
                                                     ACT.Exp), inc=SC)
        p.wait("vector", SC, se)
        p.emit("vector", lambda e: e.scalar_tensor_tensor(
            T1[:, :], TL[:, :], -1.0, T2[:, :], ALU.add, ALU.add), hs=True)
        p.emit("vector", lambda e: e.tensor_tensor(
            T2[:, :], T1[:, :], T1[:, :], ALU.mult))
        p.emit("vector", lambda e: e.tensor_reduce(
            nq16[:, :], T2[:].rearrange("p (h e) -> p h e", e=E), AX.X,
            ALU.add), hs=True)
        vr = VS.n
        p.wait("scalar", VS, vr)
        p.emit("scalar", lambda e: e.activation(rq16[:, :], nq16[:, :], ACT.Ln))
        sr = p.emit("scalar", lambda e: e.activation(
            nq16[:, :], rq16[:, :], ACT.Exp, bias=0.0, scale=-0.5), inc=SC)
        p.wait("vector", SC, sr)
        return p.emit("vector", lambda e, o=out_bf: e.tensor_tensor(
            o.rearrange("p (h e) -> p h e", e=E), bcast0(nq16, H, H, E),
            T1[:].rearrange("p (h e) -> p h e", e=E), ALU.mult), inc=VS), e2

    for st in range(NCHUNK):
        sfA = stg_free["stgA"]
        if isinstance(sfA, tuple):
            p.wait("sync", GP, sfA[1])
        else:
            p.wait("sync", VS, sfA)
        lds = dma_in(stgA[:, :], row128_ap(src_h, st))
        sfB = stg_free["stgB"]
        if isinstance(sfB, tuple):
            p.wait("sync", GP, sfB[1])
        else:
            p.wait("sync", VS, sfB)
        ldv = dma_in(stgB[:, :], row128_ap(val_h, st))
        p.wait("vector", LD, lds)
        cs = p.emit("vector", lambda e: e.tensor_copy(src_bf[:, :], stgA[:, :]),
                    inc=VS)
        stg_free["stgA"] = cs
        rdy = trans_seq(lambda j: src_bf[:, 128 * j:128 * j + 128],
                        lambda: srcT[:, :, :], NIC, after=cs)
        kt = proj_matmuls(psA, lambda j: srcT[:, j, :], wk, bk_r, (SC, rdy))
        p.wait("vector", TS, kt)
        ek, psa_free = elu_norm(psA, kn_bf)
        free["psA"] = (VS, psa_free)
        trans_seq(lambda j: kn_bf[:, 128 * j:128 * j + 128],
                  lambda st=st: bass.AP(knT, 128 * st,
                                        [[NIC * S, 128], [S, NIC], [1, 128]]),
                  NIC, after=ek)
        p.wait("vector", LD, ldv)
        cv2 = p.emit("vector", lambda e: e.tensor_copy(src_bf[:, :], stgB[:, :]),
                     inc=VS)
        stg_free["stgB"] = cv2
        rdyv = trans_seq(lambda j: src_bf[:, 128 * j:128 * j + 128],
                         lambda: valT2[:, :, :], NIC, after=cv2)
        vt = proj_matmuls(psB, lambda j: valT[:, j, :], wv, bv_r, (SC, rdyv))
        p.wait("vector", TS, vt)
        p.wait("vector", VS, ones_v)
        vv = p.emit("vector", lambda e, st=st: e.tensor_copy(
            bass.AP(vplus, st * H * 65,
                    [[NCHUNK * H * 65, 128], [65, H], [1, E]]),
            psB[:].rearrange("p (h e) -> p h e", e=E)), inc=VS)
        free["psB"] = (VS, vv)

    if dbg:
        p.wait("gpsimd", VS, VS.n)
        p.wait("gpsimd", TS, TS.n)
        p.emit("gpsimd", lambda e: e.dma_start(out=dbg_wdT, in_=wdT[:, :, :]),
               inc=ST, k=16)
    pre_done_v, pre_done_t = VS.n, TS.n
    pre.close()

    xA = sb("xA", [128, NSUB, D], F32)
    xB = sb("xB", [128, NSUB, D], F32)
    xdT = sb("xdT", [128, NIC, LTILE], BF16)
    lnT = sb("lnT", [128, NIC, LTILE], BF16)
    qnT = lnT
    ctx_bf = xdT
    PT = sb("PT", [128, 1, 1024], BF16)

    for e in ENGINES:
        p.wait(e, VS, pre_done_v)
        p.wait(e, TS, pre_done_t)

    dbg_done = [False]

    def ffn_phase(xsrc, xdst):
        pend = []

        def flush_epilogue():
            ps, s, mm = pend.pop(0)
            p.wait("vector", TS, mm)
            wait_free("vector", ("x", id(xdst)))
            ev = p.emit("vector", lambda e, ps=ps, s=s, xs=xsrc, xd=xdst:
                        e.scalar_tensor_tensor(xd[:, s, :], ps[:, 0:1024], 0.5,
                                               xs[:, s, :], ALU.mult, ALU.add),
                        inc=VS)
            free[psname[id(ps)]] = (VS, ev)
        for s in range(NSUB):
            wait_free("vector", ("ep", s))
            layernorm(xsrc[:, s, :], 1e-5, TL[:, :])
            qt = quantize(TL, xd_bf[:].rearrange("p (a b) -> p a b", b=32))
            if dbg and not dbg_done[0]:
                p.wait("gpsimd", VS, qt)
                p.emit("gpsimd", lambda e: e.dma_start(out=dbg_ln, in_=TL[:, :]),
                       inc=ST, k=16)
                p.wait("gpsimd", GP, qt)
                p.emit("gpsimd", lambda e: e.dma_start(out=dbg_xd, in_=xd_bf[:, :]),
                       inc=ST, k=16)
                dbg_st = ST.n
                free[("dbghold")] = (ST, dbg_st)
            rdy = trans_seq(lambda j: xd_bf[:, 128 * j:128 * j + 128],
                            lambda s=s: bass.AP(xdT, 128 * s,
                                                [[NIC * LTILE, 128],
                                                 [LTILE, NIC], [1, 128]]),
                            NIC, after=(GP, qt))
            free["xdbuf"] = (TS, TS.n)
            if dbg and not dbg_done[0]:
                p.wait("gpsimd", SC, rdy)
                p.emit("gpsimd", lambda e: e.dma_start(
                    out=dbg_xdT, in_=xdT[:, :, 0:128]), inc=ST, k=16)
                p.wait("vector", ST, ST.n)
                dbg_done[0] = True
            ps = psA if s % 2 == 0 else psB
            mm = proj_matmuls(ps, lambda j, s=s: xdT[:, j, 128 * s:128 * s + 128],
                              wdT, fb_r, (SC, rdy))
            pend.append((ps, s, mm))
            if len(pend) >= 2:
                flush_epilogue()
        while pend:
            flush_epilogue()
        return

    dbg2_done = [False]

    def q_phase(x1):
        pend = []

        def flush_elu():
            ps, s, mm = pend.pop(0)
            p.wait("vector", TS, mm)
            wait_free("vector", "qn_bf")
            eq, ps_free = elu_norm(ps, qn_bf)
            free[psname[id(ps)]] = (VS, ps_free)
            qdone = trans_seq(lambda j: qn_bf[:, 128 * j:128 * j + 128],
                              lambda s=s: bass.AP(qnT, 128 * s,
                                                  [[NIC * LTILE, 128],
                                                   [LTILE, NIC], [1, 128]]),
                              NIC, after=eq)
            free["qn_bf"] = (SC, qdone)
        for s in range(NSUB):
            layernorm(x1[:, s, :], 1e-5, TL[:, :])
            cb = p.emit("vector", lambda e: e.tensor_copy(ln_bf[:, :], TL[:, :]))
            rdy = trans_seq(lambda j: ln_bf[:, 128 * j:128 * j + 128],
                            lambda s=s: bass.AP(lnT, 128 * s,
                                                [[NIC * LTILE, 128],
                                                 [LTILE, NIC], [1, 128]]),
                            NIC, after=cb)
            ps = psA if s % 2 == 0 else psB
            mm = proj_matmuls(ps, lambda j, s=s: lnT[:, j, 128 * s:128 * s + 128],
                              wq, bq_r, (SC, rdy))
            pend.append((ps, s, mm))
            if len(pend) >= 2:
                flush_elu()
        while pend:
            flush_elu()
        return SC.n

    def attn_phase(x1, x2, qn_done):
        st_dma = []
        p.wait("tensor", SC, qn_done)
        for pr in range(H // 2):
            h0, h1 = 2 * pr, 2 * pr + 1
            cmm = None
            ex_t = {}
            for c in range(NCHUNK + 1):
                if c < NCHUNK:
                    ps = psA if c % 2 == 0 else psB
                    ptb = PT[:, 0, :] if c % 2 == 0 else qn_bf[:, 0:1024]
                    wait_free("tensor", psname[id(ps)])
                    p.emit("tensor", lambda e, ps=ps, pr=pr, c=c: e.matmul(
                        ps[:, 0:512], knT[0:64, pr, 128 * c:128 * c + 128],
                        qnT[0:64, pr, :], start=True, stop=True))
                    smm = p.emit("tensor", lambda e, ps=ps, pr=pr, c=c: e.matmul(
                        ps[:, 512:1024], knT[64:128, pr, 128 * c:128 * c + 128],
                        qnT[64:128, pr, :], start=True, stop=True), inc=TS)
                    p.wait("scalar", TS, smm)
                    wait_free("scalar", ("PT", c % 2))
                    ex = p.emit("scalar", lambda e, o=ptb, ps=ps: e.activation(
                        o, ps[:, 0:1024], ACT.Exp, bias=0.0,
                        scale=ATT_SCALE), inc=SC)
                    free[psname[id(ps)]] = (SC, ex)
                    ex_t[c] = ex
                if c >= 1:
                    cc = c - 1
                    pcb = PT[:, 0, :] if cc % 2 == 0 else qn_bf[:, 0:1024]
                    p.wait("tensor", SC, ex_t[cc])
                    if cc == 0:
                        wait_free("tensor", "ctxps")
                    p.emit("tensor", lambda e, cc=cc, h0=h0, o=pcb: e.matmul(
                        ctxps[0:65, 0:512], vplus[:, cc, 65 * h0:65 * h0 + 65],
                        o[:, 0:512], start=(cc == 0), stop=(cc == NCHUNK - 1)))
                    cmm = p.emit("tensor", lambda e, cc=cc, h1=h1, o=pcb: e.matmul(
                        ctxps[0:65, 512:1024],
                        vplus[:, cc, 65 * h1:65 * h1 + 65],
                        o[:, 512:1024], start=(cc == 0),
                        stop=(cc == NCHUNK - 1)), inc=TS)
                    free[("PT", cc % 2)] = (TS, cmm)
            p.wait("vector", TS, cmm)
            p.wait("scalar", TS, cmm)
            p.emit("scalar", lambda e: e.activation(
                TL[0:1, 0:1024], ctxps[64:65, 0:1024], ACT.Ln))
            rb = p.emit("scalar", lambda e: e.activation(
                ln_bf[0:1, 0:1024], TL[0:1, 0:1024], ACT.Exp, bias=0.0,
                scale=-1.0), inc=SC)
            ct = p.emit("vector", lambda e: e.tensor_copy(
                T1[0:64, 0:1024], ctxps[0:64, 0:1024]), inc=VS)
            free["ctxps"] = (VS, ct)
            if dbg2 and not dbg2_done[0]:
                p.wait("gpsimd", VS, ct)
                p.emit("gpsimd", lambda e: e.dma_start(out=dbg_ctx,
                                                       in_=T1[0:64, 0:1024]),
                       inc=ST, k=16)
                p.emit("gpsimd", lambda e: e.dma_start(out=dbg_rs,
                                                       in_=TL[0:1, 0:1024]),
                       inc=ST, k=16)
                p.wait("vector", ST, ST.n)
                dbg2_done[0] = True
            p.wait("tensor", SC, rb)
            p.wait("tensor", VS, ct)
            wait_free("tensor", "psA")
            p.emit("tensor", lambda e: e.matmul(
                psA[0:64, 0:512], ones_row[0:1, 0:64], ln_bf[0:1, 0:512],
                start=True, stop=True))
            bc = p.emit("tensor", lambda e: e.matmul(
                psA[0:64, 512:1024], ones_row[0:1, 0:64], ln_bf[0:1, 512:1024],
                start=True, stop=True), inc=TS)
            p.wait("vector", TS, bc)
            p.emit("vector", lambda e, pr=pr: e.tensor_tensor(
                ctx_bf[0:64, pr, :], T1[0:64, 0:512], psA[0:64, 0:512],
                ALU.mult))
            wait_free("vector", "stage_odd")
            nv = p.emit("vector", lambda e: e.tensor_tensor(
                stage_bf, T1[0:64, 512:1024], psA[0:64, 512:1024],
                ALU.mult), inc=VS)
            free["psA"] = (VS, nv)
            p.wait("gpsimd", VS, nv)
            sdm = p.emit("gpsimd", lambda e, pr=pr: e.dma_start(
                out=ctx_bf[64:128, pr, :], in_=stage_bf), inc=ST, k=16)
            free["stage_odd"] = (ST, sdm)
            st_dma.append(sdm)
        ctx_done = VS.n
        p.wait("tensor", ST, st_dma[-1])
        pend = []

        def flush_out():
            ps, s, mm = pend.pop(0)
            p.wait("vector", TS, mm)
            wait_free("vector", "x2gp")
            layernorm(x1[:, s, :], 1e-5, TL[:, :])
            ev1 = p.emit("vector", lambda e, ps=ps: e.tensor_tensor(
                T1[:, :], ps[:, 0:1024], TL[:, :], ALU.add), inc=VS)
            free[psname[id(ps)]] = (VS, ev1)
            lnv = layernorm(T1[:, :], 1e-12, T2[:, :])
            p.wait("gpsimd", VS, lnv)
            p.emit("gpsimd", lambda e: e.tensor_tensor(
                TL[:, :], T2[:, :], g_rep[:, :], ALU.mult))
            p.emit("gpsimd", lambda e: e.tensor_tensor(
                T2[:, :], TL[:, :], b_rep[:, :], ALU.add))
            wait_free("gpsimd", ("x", id(x2)))
            gp_t = p.emit("gpsimd", lambda e, s=s, x1=x1, x2=x2: e.tensor_tensor(
                x2[:, s, :], x1[:, s, :], T2[:, :], ALU.add), inc=GP)
            free["x2gp"] = (GP, gp_t)
            free[("ep", s)] = (GP, gp_t)
        for s in range(NSUB):
            ps = psA if s % 2 == 0 else psB
            mm = proj_matmuls(ps, lambda j, s=s:
                              ctx_bf[:, j, 128 * s:128 * s + 128],
                              wo, bo_r, ctx_done)
            pend.append((ps, s, mm))
            if len(pend) >= 2:
                flush_out()
        while pend:
            flush_out()

    for lt in range(NT):
        wait_free("sync", ("x", id(xA)))
        ldx = dma_in(xA[:], xin_tile_ap(lt))
        p.wait("vector", LD, ldx)
        ffn_phase(xA, xB)
        if stage >= 2:
            qn_done = q_phase(xB)
            attn_phase(xB, xA, qn_done)
            if stage >= 3:
                ffn_phase(xA, xB)
        out_buf = xA if stage == 2 else xB
        p.wait("gpsimd", VS, VS.n)
        p.wait("gpsimd", GP, GP.n)
        stt = p.emit("gpsimd", lambda e, lt=lt, ob=out_buf: e.dma_start(
            out=yout_tile_ap(lt), in_=ob[:]), inc=ST, k=16)
        free[("x", id(xA))] = (ST, stt)
        free[("x", id(xB))] = (ST, stt)

    p.wait("gpsimd", ST, ST.n)

    p.check_deadlock()
    with nc.Block() as block:
        @block.sync
        def _(e):
            for fn in p.ops["sync"]:
                fn(e)

        @block.gpsimd
        def _(e):
            for fn in p.ops["gpsimd"]:
                fn(e)

        @block.tensor
        def _(e):
            for fn in p.ops["tensor"]:
                fn(e)

        @block.vector
        def _(e):
            for fn in p.ops["vector"]:
                fn(e)

        @block.scalar
        def _(e):
            for fn in p.ops["scalar"]:
                fn(e)

    es.close()
    return nc


def kernel(L0_attn_target_embedding, L0_attn_source_embedding,
           L0_attn_value_embedding, L0_attn_Wq, L0_attn_bq, L0_attn_Wk,
           L0_attn_bk, L0_attn_Wv, L0_attn_bv, L0_attn_Wo, L0_attn_bo,
           L0_attn_ln_g, L0_attn_ln_b, L0_ffn_weight, L0_ffn_bias):
    x = np.ascontiguousarray(np.asarray(L0_attn_target_embedding, dtype=np.float32))
    B, L, Dm = x.shape
    shared = {
        "src": np.ascontiguousarray(np.asarray(L0_attn_source_embedding, np.float32)),
        "val": np.ascontiguousarray(np.asarray(L0_attn_value_embedding, np.float32)),
        "wq": np.ascontiguousarray(np.asarray(L0_attn_Wq, np.float32)),
        "wk": np.ascontiguousarray(np.asarray(L0_attn_Wk, np.float32)),
        "wv": np.ascontiguousarray(np.asarray(L0_attn_Wv, np.float32)),
        "wo": np.ascontiguousarray(np.asarray(L0_attn_Wo, np.float32)),
        "wd": np.ascontiguousarray(np.asarray(L0_ffn_weight, np.float32)),
        "bq": np.ascontiguousarray(np.asarray(L0_attn_bq, np.float32).reshape(1, -1)),
        "bk": np.ascontiguousarray(np.asarray(L0_attn_bk, np.float32).reshape(1, -1)),
        "bv": np.ascontiguousarray(np.asarray(L0_attn_bv, np.float32).reshape(1, -1)),
        "bo": np.ascontiguousarray(np.asarray(L0_attn_bo, np.float32).reshape(1, -1)),
        "fb": np.ascontiguousarray(np.asarray(L0_ffn_bias, np.float32).reshape(1, -1)),
        "g_rep": np.ascontiguousarray(np.broadcast_to(
            np.asarray(L0_attn_ln_g, np.float32), (128, Dm))),
        "b_rep": np.ascontiguousarray(np.broadcast_to(
            np.asarray(L0_attn_ln_b, np.float32), (128, Dm))),
        "ident": np.eye(128, dtype=ml_dtypes.bfloat16),
    }
    nc = build_nc(stage=3)
    in_maps = []
    for c in range(NCORE):
        m = dict(shared)
        m["xin"] = np.ascontiguousarray(x[2 * c:2 * c + 2].reshape(LCORE, Dm))
        in_maps.append(m)
    res = run_bass_kernel_spmd(nc, in_maps, core_ids=list(range(NCORE)))
    out = np.empty((B, L, Dm), dtype=np.float32)
    for c in range(NCORE):
        out[2 * c:2 * c + 2] = res.results[c]["yout"].reshape(2, L, Dm)
    return out


# version 06 (final)
# speedup vs baseline: 1.0672x; 1.0672x over previous
import numpy as np
import ml_dtypes
from contextlib import ExitStack

import concourse.bass as bass
import concourse.mybir as mybir
from concourse.bass_utils import run_bass_kernel_spmd

F32 = mybir.dt.float32
I32 = mybir.dt.int32
BF16 = mybir.dt.bfloat16
ALU = mybir.AluOpType
ACT = mybir.ActivationFunctionType
AX = mybir.AxisListType

NCORE = 8
D = 1024
H = 16
E = 64
S = 2048
LCORE = 4096
LTILE = 512
NSUB = LTILE // 128
NT = LCORE // LTILE
NCHUNK = S // 128
NIC = D // 128
ATT_SCALE = 0.125

ENGINES = ("sync", "gpsimd", "tensor", "vector", "scalar")


class Ctr:
    __slots__ = ("sem", "n", "name")

    def __init__(self, sem, name):
        self.sem = sem
        self.n = 0
        self.name = name


class Plan:
    def __init__(self, nc):
        self.nc = nc
        self.ops = {e: [] for e in ENGINES}
        self.trace = {e: [] for e in ENGINES}
        self.hs_ctr = None

    def emit(self, eng, fn, inc=None, k=1, hs=False):
        if eng == "vector" and self.hs_ctr is not None and inc is None:
            inc = self.hs_ctr
        hs = hs and eng == "vector"
        tgt = None
        if inc is not None:
            inc.n += k
            tgt = inc.n

            def wrapped(e, fn=fn, inc=inc, k=k):
                fn(e).then_inc(inc.sem, k)

            self.ops[eng].append(wrapped)
            self.trace[eng].append(("inc", inc.name, k))
        else:
            self.ops[eng].append(fn)
            self.trace[eng].append(("op",))
        if hs:
            self.wait(eng, inc, tgt)
        return tgt

    def wait(self, eng, ctr, val):
        if val is None or val <= 0:
            return

        def fn(e, sem=ctr.sem, val=val):
            e.wait_ge(sem, val)

        self.ops[eng].append(fn)
        self.trace[eng].append(("wait", ctr.name, val))

    def check_deadlock(self):
        pos = {e: 0 for e in ENGINES}
        sems = {}
        progressed = True
        while progressed:
            progressed = False
            for e in ENGINES:
                tr = self.trace[e]
                while pos[e] < len(tr):
                    it = tr[pos[e]]
                    if it[0] == "wait" and sems.get(it[1], 0) < it[2]:
                        break
                    if it[0] == "inc":
                        sems[it[1]] = sems.get(it[1], 0) + it[2]
                    pos[e] += 1
                    progressed = True
        stuck = {e: (pos[e], len(self.trace[e]), self.trace[e][pos[e]])
                 for e in ENGINES if pos[e] < len(self.trace[e])}
        if stuck:
            raise RuntimeError(f"deadlock in plan: {stuck}; sems={sems}")


def build_nc(stage=3):
    nc = bass.Bass("TRN2", target_bir_lowering=False)

    xin = nc.dram_tensor("xin", [LCORE, D], F32, kind="ExternalInput")
    src_h = nc.dram_tensor("src", [S, D], F32, kind="ExternalInput")
    val_h = nc.dram_tensor("val", [S, D], F32, kind="ExternalInput")
    wq_h = nc.dram_tensor("wq", [D, D], F32, kind="ExternalInput")
    wk_h = nc.dram_tensor("wk", [D, D], F32, kind="ExternalInput")
    wv_h = nc.dram_tensor("wv", [D, D], F32, kind="ExternalInput")
    wo_h = nc.dram_tensor("wo", [D, D], F32, kind="ExternalInput")
    wd_h = nc.dram_tensor("wd", [D, D], F32, kind="ExternalInput")
    bq_h = nc.dram_tensor("bq", [1, D], F32, kind="ExternalInput")
    bk_h = nc.dram_tensor("bk", [1, D], F32, kind="ExternalInput")
    bv_h = nc.dram_tensor("bv", [1, D], F32, kind="ExternalInput")
    bo_h = nc.dram_tensor("bo", [1, D], F32, kind="ExternalInput")
    fb_h = nc.dram_tensor("fb", [1, D], F32, kind="ExternalInput")
    g_h = nc.dram_tensor("g_rep", [128, D], F32, kind="ExternalInput")
    b_h = nc.dram_tensor("b_rep", [128, D], F32, kind="ExternalInput")
    id_h = nc.dram_tensor("ident", [128, 128], BF16, kind="ExternalInput")
    yout = nc.dram_tensor("yout", [LCORE, D], F32, kind="ExternalOutput")
    dbg = stage == 1
    if dbg:
        dbg_ln = nc.dram_tensor("dbg_ln", [128, D], F32, kind="ExternalOutput").ap()
        dbg_xd = nc.dram_tensor("dbg_xd", [128, D], BF16, kind="ExternalOutput").ap()
        dbg_xdT = nc.dram_tensor("dbg_xdT", [128, NIC, 128], BF16, kind="ExternalOutput").ap()
        dbg_wdT = nc.dram_tensor("dbg_wdT", [128, NIC, D], BF16, kind="ExternalOutput").ap()
    dbg2 = stage == 2
    if dbg2:
        dbg_qn = nc.dram_tensor("dbg_qn", [128, D], BF16, kind="ExternalOutput").ap()
        dbg_knT = nc.dram_tensor("dbg_knT", [128, 128], BF16, kind="ExternalOutput").ap()
        dbg_pt = nc.dram_tensor("dbg_pt", [128, D], BF16, kind="ExternalOutput").ap()
        dbg_ctx = nc.dram_tensor("dbg_ctx", [64, D], F32, kind="ExternalOutput").ap()
        dbg_rs = nc.dram_tensor("dbg_rs", [1, D], F32, kind="ExternalOutput").ap()
        dbg_vp = nc.dram_tensor("dbg_vp", [128, 130], BF16, kind="ExternalOutput").ap()

    def xin_tile_ap(lt):
        return bass.AP(xin, lt * LTILE * D, [[D, 128], [128 * D, NSUB], [1, D]])

    def yout_tile_ap(lt):
        return bass.AP(yout, lt * LTILE * D, [[D, 128], [128 * D, NSUB], [1, D]])

    def row128_ap(h, i):
        return bass.AP(h, 128 * i * D, [[D, 128], [1, D]])

    es = ExitStack()
    sb = lambda nm, shp, dt=F32: es.enter_context(nc.sbuf_tensor(nm, shp, dt))
    mkpsum = lambda nm, shp, dt=F32: es.enter_context(nc.psum_tensor(nm, shp, dt))

    ident = sb("ident_t", [128, 128], BF16)
    wq = sb("wq_t", [128, NIC, D], BF16)
    wo = sb("wo_t", [128, NIC, D], BF16)
    wdT = sb("wdT_t", [128, NIC, D], BF16)
    knT = sb("knT_t", [128, NIC, S], BF16)
    vplus = sb("vplus_t", [128, NCHUNK, H * 65], BF16)
    g_rep = sb("g_rep_t", [128, D], F32)
    b_rep = sb("b_rep_t", [128, D], F32)
    ones_row = sb("ones_row", [1, 128], BF16)
    bq_r = sb("bq_r", [1, D], BF16)
    bk_r = sb("bk_r", [1, D], BF16)
    bv_r = sb("bv_r", [1, D], BF16)
    bo_r = sb("bo_r", [1, D], BF16)
    fb_r = sb("fb_r", [1, D], BF16)
    TL = sb("TL", [128, D], F32)
    T1 = sb("T1", [128, D], F32)
    T2 = sb("T2", [128, D], F32)
    I1 = sb("I1", [128, D], I32)
    I2 = sb("I2", [128, D], I32)
    AMX = sb("AMX", [128, 32], F32)
    SCL = sb("SCL", [128, 32], F32)
    RS8 = sb("RS8", [128, 32], F32)
    SC8 = sb("SC8", [128, 32], F32)
    stat6 = sb("stat6", [128, 12], F32)
    mv = sb("mv", [128, 2], F32)
    sd1 = sb("sd1", [128, 1], F32)
    rstd = sb("rstd", [128, 1], F32)
    nq16 = sb("nq16", [128, H], F32)
    rq16 = sb("rq16", [128, H], F32)
    xd_bf = sb("xd_bf", [128, D], BF16)
    ln_bf = sb("ln_bf", [128, D], BF16)
    qn_bf = sb("qn_bf", [128, D], BF16)
    stage_bf = ln_bf[64:128, 0:512]

    psA = mkpsum("psA", [128, 1024], F32)
    psB = mkpsum("psB", [128, 1024], F32)
    ctxps = mkpsum("ctxps", [128, 1024], F32)
    trpsA = mkpsum("trpsA", [128, 1024], BF16)
    trpsB = mkpsum("trpsB", [128, 1024], BF16)
    psname = {id(psA): "psA", id(psB): "psB", id(ctxps): "ctxps"}

    p = Plan(nc)
    LD = Ctr(es.enter_context(nc.semaphore("LD")), "LD")
    ST = Ctr(es.enter_context(nc.semaphore("ST")), "ST")
    TS = Ctr(es.enter_context(nc.semaphore("TS")), "TS")
    VS = Ctr(es.enter_context(nc.semaphore("VS")), "VS")
    SC = Ctr(es.enter_context(nc.semaphore("SC")), "SC")
    GP = Ctr(es.enter_context(nc.semaphore("GP")), "GP")
    p.hs_ctr = VS

    free = {}

    def wait_free(eng, key):
        if key in free and free[key] is not None:
            ctr, val = free[key]
            p.wait(eng, ctr, val)

    def dma_in(out_ap, in_ap):
        return p.emit("sync", lambda e, o=out_ap, i=in_ap:
                      e.dma_start(out=o, in_=i), inc=LD, k=16)

    def bcast0(t, nalloc, n, inner):
        return bass.AP(t, 0, [[nalloc, 128], [1, n], [0, inner]])

    tr_par = [0]

    def trans_seq(src_fn, dst3_fn, n, after=None):
        par = tr_par[0]
        tr_par[0] ^= 1
        trp = trpsA if par == 0 else trpsB
        if after is not None:
            if isinstance(after, tuple):
                p.wait("tensor", after[0], after[1])
            else:
                p.wait("tensor", VS, after)
        wait_free("tensor", ("trps", par))
        t_tgt = None
        for j in range(n):
            t_tgt = p.emit("tensor", lambda e, s=src_fn(j), o=trp, j=j:
                           e.transpose(o[:, 128 * j:128 * j + 128], s,
                                       ident[:]), inc=TS)
        p.wait("scalar", TS, t_tgt)
        s_tgt = p.emit("scalar", lambda e, d=dst3_fn(), t=trp, n=n: e.copy(
            d, t[:, 0:128 * n].rearrange("p (a b) -> p a b", b=128)), inc=SC)
        free[("trps", par)] = (SC, s_tgt)
        return s_tgt

    def layernorm(src_ap, eps, out_ap):
        p.emit("vector", lambda e, s=src_ap: e.bn_stats(stat6[:, 0:6], s[:, 0:512]), hs=True)
        p.emit("vector", lambda e, s=src_ap: e.bn_stats(stat6[:, 6:12], s[:, 512:1024]), hs=True)
        p.emit("vector", lambda e: e.bn_aggr(mv[:, :], stat6[:, :]), hs=True)
        v1 = p.emit("vector", lambda e, ep=eps: e.tensor_scalar(
            sd1[:, :], mv[:, 1:2], ep, None, ALU.add), inc=VS)
        p.wait("scalar", VS, v1)
        p.emit("scalar", lambda e: e.activation(rstd[:, :], sd1[:, :], ACT.Ln))
        s1 = p.emit("scalar", lambda e: e.activation(
            sd1[:, :], rstd[:, :], ACT.Exp, bias=0.0, scale=-0.5), inc=SC)
        p.wait("vector", SC, s1)
        return p.emit("vector", lambda e, s=src_ap, o=out_ap: e.tensor_scalar(
            o, s, mv[:, 0:1], sd1[:, :], ALU.subtract, ALU.mult), inc=VS, hs=True)

    def quantize(src_t, out_bf_ap3):
        src3 = src_t[:].rearrange("p (a b) -> p a b", b=32)
        p.emit("vector", lambda e, s=src3: e.tensor_reduce(
            AMX[:, :], s, AX.X, ALU.max, apply_absolute_value=True), hs=True)
        p.emit("vector", lambda e: e.tensor_scalar(
            SCL[:, :], AMX[:, :], 1.0 / 7.5, 1e-30, ALU.mult, ALU.max), hs=True)
        p.emit("vector", lambda e: e.reciprocal(AMX[:, :], SCL[:, :]), hs=True)
        p.emit("vector", lambda e: e.tensor_scalar(
            RS8[:, :], AMX[:, :], 8.0, None, ALU.mult), hs=True)
        p.emit("vector", lambda e: e.tensor_scalar(
            SC8[:, :], SCL[:, :], 0.125, None, ALU.mult), hs=True)
        t1_3 = T1[:].rearrange("p (a b) -> p a b", b=32)
        p.emit("vector", lambda e, s=src3: e.tensor_tensor(
            t1_3, bcast0(RS8, 32, 32, 32), s, ALU.mult))
        p.emit("vector", lambda e: e.tensor_scalar(
            I1[:, :], T1[:, :], 16.0, -16.0, ALU.min, ALU.max), hs=True)
        p.emit("vector", lambda e: e.tensor_scalar(
            T2[:, :], T1[:, :], 0.5, None, ALU.mult), hs=True)
        p.emit("vector", lambda e: e.tensor_scalar(
            T1[:, :], T2[:, :], 16.0, -16.0, ALU.min, ALU.max), hs=True)
        p.emit("vector", lambda e: e.tensor_scalar(
            TL[:, :], T2[:, :], 8.0, -8.0, ALU.min, ALU.max), hs=True)
        p.emit("vector", lambda e: e.tensor_tensor(
            I2[:, :], T1[:, :], TL[:, :], ALU.subtract))
        p.emit("vector", lambda e: e.tensor_scalar(
            T1[:, :], T2[:, :], 0.5, None, ALU.mult), hs=True)
        p.emit("vector", lambda e: e.tensor_scalar(
            TL[:, :], T1[:, :], 8.0, -8.0, ALU.min, ALU.max), hs=True)
        p.emit("vector", lambda e: e.scalar_tensor_tensor(
            T2[:, :], I2[:, :], 2.0, I1[:, :], ALU.mult, ALU.add), hs=True)
        p.emit("vector", lambda e: e.tensor_tensor(
            I1[:, :], T1[:, :], TL[:, :], ALU.subtract))
        p.emit("vector", lambda e: e.scalar_tensor_tensor(
            T1[:, :], I1[:, :], 4.0, T2[:, :], ALU.mult, ALU.add), hs=True)
        return p.emit("vector", lambda e, o=out_bf_ap3: e.tensor_tensor(
            o, bcast0(SC8, 32, 32, 32), t1_3, ALU.mult), inc=VS)

    def proj_matmuls(ps, lhsT_fn, rhs_t, bias_r, operands_ready):
        wait_free("tensor", psname[id(ps)])
        if isinstance(operands_ready, tuple):
            p.wait("tensor", operands_ready[0], operands_ready[1])
        else:
            p.wait("tensor", VS, operands_ready)
        tgt = None
        for half in range(2):
            lo = 512 * half
            for j in range(NIC):
                p.emit("tensor", lambda e, ps=ps, lo=lo, l=lhsT_fn(j),
                       r=rhs_t[:, j, lo:lo + 512], st=(j == 0):
                       e.matmul(ps[:, lo:lo + 512], l, r, start=st, stop=False))
            tgt = p.emit("tensor", lambda e, ps=ps, lo=lo, b=bias_r:
                         e.matmul(ps[:, lo:lo + 512], ones_row[0:1, :],
                                  b[:, lo:lo + 512], start=False, stop=True),
                         inc=TS)
        return tgt

    id_ld = dma_in(ident[:], id_h.ap())
    dma_in(g_rep[:], g_h.ap())
    dma_in(b_rep[:], b_h.ap())
    p.wait("tensor", LD, id_ld)

    for hsrc, dstr in [(bq_h, bq_r), (bk_h, bk_r), (bv_h, bv_r),
                       (bo_h, bo_r), (fb_h, fb_r)]:
        p.emit("gpsimd", lambda e, i=hsrc.ap(), o=dstr[:, :]:
               e.dma_start(out=o, in_=i), inc=ST, k=16)
    bias_ld = ST.n
    p.wait("tensor", ST, bias_ld)

    p.emit("vector", lambda e: e.memset(ones_row[:, :], 1.0))
    ones_v = p.emit("vector", lambda e: e.memset(
        bass.AP(vplus, 64, [[NCHUNK * H * 65, 128], [H * 65, NCHUNK], [65, H]]),
        1.0), inc=VS)

    xdT = sb("xdT", [128, NIC, LTILE], BF16)
    lnT = sb("lnT", [128, NIC, LTILE], BF16)
    qnT = lnT
    ctx_bf = xdT
    PT = sb("PT", [128, 1, 1024], BF16)

    pre = ExitStack()
    wk = pre.enter_context(nc.sbuf_tensor("wk_t", [128, NIC, D], BF16))
    wv = pre.enter_context(nc.sbuf_tensor("wv_t", [128, NIC, D], BF16))
    wdq_c = PT[:, 0, :]
    srcT = ln_bf[:].rearrange("p (a b) -> p a b", b=128)
    src_bf = qn_bf
    kn_bf = xd_bf
    valT = kn_bf[:].rearrange("p (a b) -> p a b", b=128)
    valT2 = valT

    for wi, (h, dst) in enumerate([(wq_h, wq), (wk_h, wk), (wv_h, wv),
                                   (wo_h, wo)]):
        for i in range(NIC):
            p.emit("gpsimd", lambda e, i_=row128_ap(h, i), o=dst[:, i, :]:
                   e.dma_start(out=o, in_=i_), inc=ST, k=16)
    w_ld = ST.n
    for oc in range(NIC):
        ld = dma_in(TL[:, :], row128_ap(wd_h, oc))
        p.wait("vector", LD, ld)
        qv = quantize(TL, wdq_c.rearrange("p (a b) -> p a b", b=32))
        trans_seq(lambda j: wdq_c[:, 128 * j:128 * j + 128],
                  lambda oc=oc: bass.AP(wdT, 128 * oc,
                                        [[NIC * D, 128], [D, NIC], [1, 128]]),
                  NIC, after=qv)
        free["wdld"] = (VS, qv)
        wait_free("sync", "wdld")

    def elu_norm(src_ps, out_bf):
        p.emit("vector", lambda e, ps=src_ps: e.tensor_scalar(
            T1[:, :], ps[:, 0:1024], 0.0, None, ALU.min), hs=True)
        e2 = p.emit("vector", lambda e, ps=src_ps: e.tensor_scalar(
            T2[:, :], ps[:, 0:1024], 0.0, None, ALU.max), inc=VS, hs=True)
        p.wait("scalar", VS, e2)
        se = p.emit("scalar", lambda e: e.activation(TL[:, :], T1[:, :],
                                                     ACT.Exp), inc=SC)
        p.wait("vector", SC, se)
        p.emit("vector", lambda e: e.scalar_tensor_tensor(
            T1[:, :], TL[:, :], -1.0, T2[:, :], ALU.add, ALU.add), hs=True)
        p.emit("vector", lambda e: e.tensor_tensor(
            T2[:, :], T1[:, :], T1[:, :], ALU.mult))
        p.emit("vector", lambda e: e.tensor_reduce(
            nq16[:, :], T2[:].rearrange("p (h e) -> p h e", e=E), AX.X,
            ALU.add), hs=True)
        vr = VS.n
        p.wait("scalar", VS, vr)
        p.emit("scalar", lambda e: e.activation(rq16[:, :], nq16[:, :], ACT.Ln))
        sr = p.emit("scalar", lambda e: e.activation(
            nq16[:, :], rq16[:, :], ACT.Exp, bias=0.0, scale=-0.5), inc=SC)
        p.wait("vector", SC, sr)
        return p.emit("vector", lambda e, o=out_bf: e.tensor_tensor(
            o.rearrange("p (h e) -> p h e", e=E), bcast0(nq16, H, H, E),
            T1[:].rearrange("p (h e) -> p h e", e=E), ALU.mult), inc=VS), e2

    srcT_b = [lnT[:, :, 0:128], lnT[:, :, 128:256]]
    valT_s = lnT[:, :, 256:384]
    kn_b = xd_bf
    tails = []

    def kv_tail():
        st, kt, ldv, kps = tails.pop(0)
        p.wait("vector", TS, kt)
        wait_free("vector", "knb")
        ek, kps_free = elu_norm(kps, kn_b)
        free[psname[id(kps)]] = (VS, kps_free)
        knd = trans_seq(lambda j: kn_b[:, 128 * j:128 * j + 128],
                        lambda st=st: bass.AP(knT, 128 * st,
                                              [[NIC * S, 128], [S, NIC],
                                               [1, 128]]),
                        NIC, after=ek)
        free["knb"] = (SC, knd)
        b = st % 2
        wait_free("tensor", "valT")
        rdyv = trans_seq(lambda j, b=b: (PT[:, 0, 128 * j:128 * j + 128]
                                         if b == 0 else lnT[:, j, 384:512]),
                         lambda: valT_s[:, :, :], NIC, after=(ST, ldv))
        free[("kvval", b)] = (SC, rdyv)
        vt = proj_matmuls(psB, lambda j: valT_s[:, j, :], wv, bv_r, (SC, rdyv))
        free["valT"] = (TS, vt)
        p.wait("vector", TS, vt)
        vv = p.emit("vector", lambda e, st=st: e.tensor_copy(
            bass.AP(vplus, st * H * 65,
                    [[NCHUNK * H * 65, 128], [65, H], [1, E]]),
            psB[:].rearrange("p (h e) -> p h e", e=E)), inc=VS)
        free["psB"] = (VS, vv)

    p.wait("tensor", ST, w_ld)
    free[("kvval", 0)] = (SC, SC.n)
    for st in range(NCHUNK):
        b = st % 2
        wait_free("gpsimd", ("kvsrc", b))
        lds = p.emit("gpsimd", lambda e, i_=row128_ap(src_h, st), b=b:
                     e.dma_start(out=(qn_bf if b == 0 else ln_bf)[:, :],
                                 in_=i_), inc=ST, k=16)
        wait_free("gpsimd", ("kvval", b))
        ldv = p.emit("gpsimd", lambda e, i_=row128_ap(val_h, st), b=b:
                     e.dma_start(out=(PT[:, 0, :] if b == 0
                                      else bass.AP(lnT, 384,
                                                   [[NIC * 512, 128],
                                                    [512, NIC], [1, 128]])),
                                 in_=i_), inc=ST, k=16)
        wait_free("tensor", ("srcT", b))
        rdy = trans_seq(lambda j, b=b: (qn_bf if b == 0 else ln_bf)[:, 128 * j:128 * j + 128],
                        lambda b=b: srcT_b[b][:, :, :], NIC, after=(ST, lds))
        free[("kvsrc", b)] = (SC, rdy)
        kps = psA if b == 0 else ctxps
        kt = proj_matmuls(kps, lambda j, b=b: srcT_b[b][:, j, :], wk, bk_r,
                          (SC, rdy))
        free[("srcT", b)] = (TS, kt)
        tails.append((st, kt, ldv, kps))
        if len(tails) >= 2:
            kv_tail()
    while tails:
        kv_tail()

    pre_done_v, pre_done_t = VS.n, TS.n
    pre.close()

    xA = sb("xA", [128, NSUB, D], F32)
    xB = sb("xB", [128, NSUB, D], F32)

    for e in ENGINES:
        p.wait(e, VS, pre_done_v)
        p.wait(e, TS, pre_done_t)

    dbg_done = [False]

    def ffn_phase(xsrc, xdst):
        pend = []

        def flush_epilogue():
            ps, s, mm = pend.pop(0)
            p.wait("vector", TS, mm)
            wait_free("vector", ("x", id(xdst)))
            ev = p.emit("vector", lambda e, ps=ps, s=s, xs=xsrc, xd=xdst:
                        e.scalar_tensor_tensor(xd[:, s, :], ps[:, 0:1024], 0.5,
                                               xs[:, s, :], ALU.mult, ALU.add),
                        inc=VS)
            free[psname[id(ps)]] = (VS, ev)
        for s in range(NSUB):
            layernorm(xsrc[:, s, :], 1e-5, TL[:, :])
            qt = quantize(TL, xd_bf[:].rearrange("p (a b) -> p a b", b=32))
            if dbg and not dbg_done[0]:
                p.wait("gpsimd", VS, qt)
                p.emit("gpsimd", lambda e: e.dma_start(out=dbg_ln, in_=TL[:, :]),
                       inc=ST, k=16)
                p.wait("gpsimd", VS, qt)
                p.emit("gpsimd", lambda e: e.dma_start(out=dbg_xd, in_=xd_bf[:, :]),
                       inc=ST, k=16)
                dbg_st = ST.n
                free[("dbghold")] = (ST, dbg_st)
            rdy = trans_seq(lambda j: xd_bf[:, 128 * j:128 * j + 128],
                            lambda s=s: bass.AP(xdT, 128 * s,
                                                [[NIC * LTILE, 128],
                                                 [LTILE, NIC], [1, 128]]),
                            NIC, after=qt)
            if dbg and not dbg_done[0]:
                p.wait("gpsimd", SC, rdy)
                p.emit("gpsimd", lambda e: e.dma_start(
                    out=dbg_xdT, in_=xdT[:, :, 0:128]), inc=ST, k=16)
                p.wait("vector", ST, ST.n)
                dbg_done[0] = True
            ps = psA if s % 2 == 0 else psB
            mm = proj_matmuls(ps, lambda j, s=s: xdT[:, j, 128 * s:128 * s + 128],
                              wdT, fb_r, (SC, rdy))
            pend.append((ps, s, mm))
            if len(pend) >= 2:
                flush_epilogue()
        while pend:
            flush_epilogue()
        return

    dbg2_done = [False]

    def q_phase(x1):
        pend = []

        def flush_elu():
            ps, s, mm = pend.pop(0)
            p.wait("vector", TS, mm)
            wait_free("vector", "qn_bf")
            eq, ps_free = elu_norm(ps, qn_bf)
            free[psname[id(ps)]] = (VS, ps_free)
            qdone = trans_seq(lambda j: qn_bf[:, 128 * j:128 * j + 128],
                              lambda s=s: bass.AP(qnT, 128 * s,
                                                  [[NIC * LTILE, 128],
                                                   [LTILE, NIC], [1, 128]]),
                              NIC, after=eq)
            free["qn_bf"] = (SC, qdone)
        for s in range(NSUB):
            layernorm(x1[:, s, :], 1e-5, TL[:, :])
            cb = p.emit("vector", lambda e: e.tensor_copy(ln_bf[:, :], TL[:, :]))
            rdy = trans_seq(lambda j: ln_bf[:, 128 * j:128 * j + 128],
                            lambda s=s: bass.AP(lnT, 128 * s,
                                                [[NIC * LTILE, 128],
                                                 [LTILE, NIC], [1, 128]]),
                            NIC, after=cb)
            ps = psA if s % 2 == 0 else psB
            mm = proj_matmuls(ps, lambda j, s=s: lnT[:, j, 128 * s:128 * s + 128],
                              wq, bq_r, (SC, rdy))
            pend.append((ps, s, mm))
            if len(pend) >= 2:
                flush_elu()
        while pend:
            flush_elu()
        return SC.n

    def attn_phase(x1, x2, qn_done):
        st_dma = []
        p.wait("tensor", SC, qn_done)
        for pr in range(H // 2):
            h0, h1 = 2 * pr, 2 * pr + 1
            cmm = None
            ex_t = {}
            for c in range(NCHUNK + 1):
                if c < NCHUNK:
                    ps = psA if c % 2 == 0 else psB
                    ptb = PT[:, 0, :] if c % 2 == 0 else qn_bf[:, 0:1024]
                    wait_free("tensor", psname[id(ps)])
                    p.emit("tensor", lambda e, ps=ps, pr=pr, c=c: e.matmul(
                        ps[:, 0:512], knT[0:64, pr, 128 * c:128 * c + 128],
                        qnT[0:64, pr, :], start=True, stop=True))
                    smm = p.emit("tensor", lambda e, ps=ps, pr=pr, c=c: e.matmul(
                        ps[:, 512:1024], knT[64:128, pr, 128 * c:128 * c + 128],
                        qnT[64:128, pr, :], start=True, stop=True,
                        tile_position=(64, 0)), inc=TS)
                    p.wait("scalar", TS, smm)
                    wait_free("scalar", ("PT", c % 2))
                    ex = p.emit("scalar", lambda e, o=ptb, ps=ps: e.activation(
                        o, ps[:, 0:1024], ACT.Exp, bias=0.0,
                        scale=ATT_SCALE), inc=SC)
                    free[psname[id(ps)]] = (SC, ex)
                    ex_t[c] = ex
                if c >= 1:
                    cc = c - 1
                    pcb = PT[:, 0, :] if cc % 2 == 0 else qn_bf[:, 0:1024]
                    p.wait("tensor", SC, ex_t[cc])
                    if cc == 0:
                        wait_free("tensor", "ctxps")
                    p.emit("tensor", lambda e, cc=cc, h0=h0, o=pcb: e.matmul(
                        ctxps[0:65, 0:512], vplus[:, cc, 65 * h0:65 * h0 + 65],
                        o[:, 0:512], start=(cc == 0), stop=(cc == NCHUNK - 1)))
                    cmm = p.emit("tensor", lambda e, cc=cc, h1=h1, o=pcb: e.matmul(
                        ctxps[0:65, 512:1024],
                        vplus[:, cc, 65 * h1:65 * h1 + 65],
                        o[:, 512:1024], start=(cc == 0),
                        stop=(cc == NCHUNK - 1)), inc=TS)
                    free[("PT", cc % 2)] = (TS, cmm)
            p.wait("vector", TS, cmm)
            p.wait("scalar", TS, cmm)
            p.emit("scalar", lambda e: e.activation(
                TL[0:1, 0:1024], ctxps[64:65, 0:1024], ACT.Ln))
            rb = p.emit("scalar", lambda e: e.activation(
                ln_bf[0:1, 0:1024], TL[0:1, 0:1024], ACT.Exp, bias=0.0,
                scale=-1.0), inc=SC)
            ct = p.emit("vector", lambda e: e.tensor_copy(
                T1[0:64, 0:1024], ctxps[0:64, 0:1024]), inc=VS)
            free["ctxps"] = (VS, ct)
            if dbg2 and not dbg2_done[0]:
                p.wait("gpsimd", VS, ct)
                p.emit("gpsimd", lambda e: e.dma_start(out=dbg_ctx,
                                                       in_=T1[0:64, 0:1024]),
                       inc=ST, k=16)
                p.emit("gpsimd", lambda e: e.dma_start(out=dbg_rs,
                                                       in_=TL[0:1, 0:1024]),
                       inc=ST, k=16)
                p.wait("vector", ST, ST.n)
                dbg2_done[0] = True
            p.wait("tensor", SC, rb)
            p.wait("tensor", VS, ct)
            wait_free("tensor", "psA")
            p.emit("tensor", lambda e: e.matmul(
                psA[0:64, 0:512], ones_row[0:1, 0:64], ln_bf[0:1, 0:512],
                start=True, stop=True))
            bc = p.emit("tensor", lambda e: e.matmul(
                psA[0:64, 512:1024], ones_row[0:1, 0:64], ln_bf[0:1, 512:1024],
                start=True, stop=True), inc=TS)
            p.wait("vector", TS, bc)
            p.emit("vector", lambda e, pr=pr: e.tensor_tensor(
                ctx_bf[0:64, pr, :], T1[0:64, 0:512], psA[0:64, 0:512],
                ALU.mult))
            wait_free("vector", "stage_odd")
            nv = p.emit("vector", lambda e: e.tensor_tensor(
                stage_bf, T1[0:64, 512:1024], psA[0:64, 512:1024],
                ALU.mult), inc=VS)
            free["psA"] = (VS, nv)
            p.wait("gpsimd", VS, nv)
            sdm = p.emit("gpsimd", lambda e, pr=pr: e.dma_start(
                out=ctx_bf[64:128, pr, :], in_=stage_bf), inc=ST, k=16)
            free["stage_odd"] = (ST, sdm)
            st_dma.append(sdm)
        ctx_done = VS.n
        p.wait("tensor", ST, st_dma[-1])
        pend = []

        def flush_out():
            ps, s, mm = pend.pop(0)
            p.wait("vector", TS, mm)
            layernorm(x1[:, s, :], 1e-5, TL[:, :])
            ev1 = p.emit("vector", lambda e, ps=ps: e.tensor_tensor(
                T1[:, :], ps[:, 0:1024], TL[:, :], ALU.add), inc=VS)
            free[psname[id(ps)]] = (VS, ev1)
            layernorm(T1[:, :], 1e-12, T2[:, :])
            p.emit("vector", lambda e: e.tensor_tensor(
                TL[:, :], T2[:, :], g_rep[:, :], ALU.mult))
            p.emit("vector", lambda e: e.tensor_tensor(
                T2[:, :], TL[:, :], b_rep[:, :], ALU.add))
            wait_free("vector", ("x", id(x2)))
            p.emit("vector", lambda e, s=s, x1=x1, x2=x2: e.tensor_tensor(
                x2[:, s, :], x1[:, s, :], T2[:, :], ALU.add), inc=VS)
        for s in range(NSUB):
            ps = psA if s % 2 == 0 else psB
            mm = proj_matmuls(ps, lambda j, s=s:
                              ctx_bf[:, j, 128 * s:128 * s + 128],
                              wo, bo_r, ctx_done)
            pend.append((ps, s, mm))
            if len(pend) >= 2:
                flush_out()
        while pend:
            flush_out()

    for lt in range(NT):
        wait_free("sync", ("x", id(xA)))
        ldx = dma_in(xA[:], xin_tile_ap(lt))
        p.wait("vector", LD, ldx)
        ffn_phase(xA, xB)
        if stage >= 2:
            qn_done = q_phase(xB)
            attn_phase(xB, xA, qn_done)
            if stage >= 3:
                ffn_phase(xA, xB)
        out_buf = xA if stage == 2 else xB
        p.wait("gpsimd", VS, VS.n)
        stt = p.emit("gpsimd", lambda e, lt=lt, ob=out_buf: e.dma_start(
            out=yout_tile_ap(lt), in_=ob[:]), inc=ST, k=16)
        free[("x", id(xA))] = (ST, stt)
        free[("x", id(xB))] = (ST, stt)

    p.wait("gpsimd", ST, ST.n)

    p.check_deadlock()
    with nc.Block() as block:
        @block.sync
        def _(e):
            for fn in p.ops["sync"]:
                fn(e)

        @block.gpsimd
        def _(e):
            for fn in p.ops["gpsimd"]:
                fn(e)

        @block.tensor
        def _(e):
            for fn in p.ops["tensor"]:
                fn(e)

        @block.vector
        def _(e):
            for fn in p.ops["vector"]:
                fn(e)

        @block.scalar
        def _(e):
            for fn in p.ops["scalar"]:
                fn(e)

    es.close()
    return nc


def kernel(L0_attn_target_embedding, L0_attn_source_embedding,
           L0_attn_value_embedding, L0_attn_Wq, L0_attn_bq, L0_attn_Wk,
           L0_attn_bk, L0_attn_Wv, L0_attn_bv, L0_attn_Wo, L0_attn_bo,
           L0_attn_ln_g, L0_attn_ln_b, L0_ffn_weight, L0_ffn_bias):
    x = np.ascontiguousarray(np.asarray(L0_attn_target_embedding, dtype=np.float32))
    B, L, Dm = x.shape
    shared = {
        "src": np.ascontiguousarray(np.asarray(L0_attn_source_embedding, np.float32)),
        "val": np.ascontiguousarray(np.asarray(L0_attn_value_embedding, np.float32)),
        "wq": np.ascontiguousarray(np.asarray(L0_attn_Wq, np.float32)),
        "wk": np.ascontiguousarray(np.asarray(L0_attn_Wk, np.float32)),
        "wv": np.ascontiguousarray(np.asarray(L0_attn_Wv, np.float32)),
        "wo": np.ascontiguousarray(np.asarray(L0_attn_Wo, np.float32)),
        "wd": np.ascontiguousarray(np.asarray(L0_ffn_weight, np.float32)),
        "bq": np.ascontiguousarray(np.asarray(L0_attn_bq, np.float32).reshape(1, -1)),
        "bk": np.ascontiguousarray(np.asarray(L0_attn_bk, np.float32).reshape(1, -1)),
        "bv": np.ascontiguousarray(np.asarray(L0_attn_bv, np.float32).reshape(1, -1)),
        "bo": np.ascontiguousarray(np.asarray(L0_attn_bo, np.float32).reshape(1, -1)),
        "fb": np.ascontiguousarray(np.asarray(L0_ffn_bias, np.float32).reshape(1, -1)),
        "g_rep": np.ascontiguousarray(np.broadcast_to(
            np.asarray(L0_attn_ln_g, np.float32), (128, Dm))),
        "b_rep": np.ascontiguousarray(np.broadcast_to(
            np.asarray(L0_attn_ln_b, np.float32), (128, Dm))),
        "ident": np.eye(128, dtype=ml_dtypes.bfloat16),
    }
    nc = build_nc(stage=3)
    in_maps = []
    for c in range(NCORE):
        m = dict(shared)
        m["xin"] = np.ascontiguousarray(x[2 * c:2 * c + 2].reshape(LCORE, Dm))
        in_maps.append(m)
    res = run_bass_kernel_spmd(nc, in_maps, core_ids=list(range(NCORE)))
    out = np.empty((B, L, Dm), dtype=np.float32)
    for c in range(NCORE):
        out[2 * c:2 * c + 2] = res.results[c]["yout"].reshape(2, L, Dm)
    return out


# version 07
# speedup vs baseline: 1.0914x; 1.0227x over previous
import numpy as np
import ml_dtypes
from contextlib import ExitStack

import concourse.bass as bass
import concourse.mybir as mybir
from concourse.bass_utils import run_bass_kernel_spmd

F32 = mybir.dt.float32
I32 = mybir.dt.int32
BF16 = mybir.dt.bfloat16
ALU = mybir.AluOpType
ACT = mybir.ActivationFunctionType
AX = mybir.AxisListType

NCORE = 8
D = 1024
H = 16
E = 64
S = 2048
LCORE = 4096
LTILE = 512
NSUB = LTILE // 128
NT = LCORE // LTILE
NCHUNK = S // 128
NIC = D // 128
ATT_SCALE = 0.125

ENGINES = ("sync", "gpsimd", "tensor", "vector", "scalar")


class Ctr:
    __slots__ = ("sem", "n", "name")

    def __init__(self, sem, name):
        self.sem = sem
        self.n = 0
        self.name = name


class Plan:
    def __init__(self, nc):
        self.nc = nc
        self.ops = {e: [] for e in ENGINES}
        self.trace = {e: [] for e in ENGINES}
        self.hs_ctr = None

    def emit(self, eng, fn, inc=None, k=1, hs=False):
        if eng == "vector" and self.hs_ctr is not None and inc is None:
            inc = self.hs_ctr
        hs = hs and eng == "vector"
        tgt = None
        if inc is not None:
            inc.n += k
            tgt = inc.n

            def wrapped(e, fn=fn, inc=inc, k=k):
                fn(e).then_inc(inc.sem, k)

            self.ops[eng].append(wrapped)
            self.trace[eng].append(("inc", inc.name, k))
        else:
            self.ops[eng].append(fn)
            self.trace[eng].append(("op",))
        if hs:
            self.wait(eng, inc, tgt)
        return tgt

    def wait(self, eng, ctr, val):
        if val is None or val <= 0:
            return

        def fn(e, sem=ctr.sem, val=val):
            e.wait_ge(sem, val)

        self.ops[eng].append(fn)
        self.trace[eng].append(("wait", ctr.name, val))

    def check_deadlock(self):
        pos = {e: 0 for e in ENGINES}
        sems = {}
        progressed = True
        while progressed:
            progressed = False
            for e in ENGINES:
                tr = self.trace[e]
                while pos[e] < len(tr):
                    it = tr[pos[e]]
                    if it[0] == "wait" and sems.get(it[1], 0) < it[2]:
                        break
                    if it[0] == "inc":
                        sems[it[1]] = sems.get(it[1], 0) + it[2]
                    pos[e] += 1
                    progressed = True
        stuck = {e: (pos[e], len(self.trace[e]), self.trace[e][pos[e]])
                 for e in ENGINES if pos[e] < len(self.trace[e])}
        if stuck:
            raise RuntimeError(f"deadlock in plan: {stuck}; sems={sems}")


def build_nc(stage=3):
    nc = bass.Bass("TRN2", target_bir_lowering=False)

    xin = nc.dram_tensor("xin", [LCORE, D], F32, kind="ExternalInput")
    src_h = nc.dram_tensor("src", [S, D], F32, kind="ExternalInput")
    val_h = nc.dram_tensor("val", [S, D], F32, kind="ExternalInput")
    wq_h = nc.dram_tensor("wq", [D, D], F32, kind="ExternalInput")
    wk_h = nc.dram_tensor("wk", [D, D], F32, kind="ExternalInput")
    wv_h = nc.dram_tensor("wv", [D, D], F32, kind="ExternalInput")
    wo_h = nc.dram_tensor("wo", [D, D], F32, kind="ExternalInput")
    wd_h = nc.dram_tensor("wd", [D, D], F32, kind="ExternalInput")
    bq_h = nc.dram_tensor("bq", [1, D], F32, kind="ExternalInput")
    bk_h = nc.dram_tensor("bk", [1, D], F32, kind="ExternalInput")
    bv_h = nc.dram_tensor("bv", [1, D], F32, kind="ExternalInput")
    bo_h = nc.dram_tensor("bo", [1, D], F32, kind="ExternalInput")
    fb_h = nc.dram_tensor("fb", [1, D], F32, kind="ExternalInput")
    g_h = nc.dram_tensor("g_rep", [128, D], F32, kind="ExternalInput")
    b_h = nc.dram_tensor("b_rep", [128, D], F32, kind="ExternalInput")
    id_h = nc.dram_tensor("ident", [128, 128], BF16, kind="ExternalInput")
    yout = nc.dram_tensor("yout", [LCORE, D], F32, kind="ExternalOutput")
    dbg = stage == 1
    if dbg:
        dbg_ln = nc.dram_tensor("dbg_ln", [128, D], F32, kind="ExternalOutput").ap()
        dbg_xd = nc.dram_tensor("dbg_xd", [128, D], BF16, kind="ExternalOutput").ap()
        dbg_xdT = nc.dram_tensor("dbg_xdT", [128, NIC, 128], BF16, kind="ExternalOutput").ap()
        dbg_wdT = nc.dram_tensor("dbg_wdT", [128, NIC, D], BF16, kind="ExternalOutput").ap()
    dbg2 = stage == 2
    if dbg2:
        dbg_qn = nc.dram_tensor("dbg_qn", [128, D], BF16, kind="ExternalOutput").ap()
        dbg_knT = nc.dram_tensor("dbg_knT", [128, 128], BF16, kind="ExternalOutput").ap()
        dbg_pt = nc.dram_tensor("dbg_pt", [128, D], BF16, kind="ExternalOutput").ap()
        dbg_ctx = nc.dram_tensor("dbg_ctx", [64, D], F32, kind="ExternalOutput").ap()
        dbg_rs = nc.dram_tensor("dbg_rs", [1, D], F32, kind="ExternalOutput").ap()
        dbg_vp = nc.dram_tensor("dbg_vp", [128, 130], BF16, kind="ExternalOutput").ap()

    def xin_tile_ap(lt):
        return bass.AP(xin, lt * LTILE * D, [[D, 128], [128 * D, NSUB], [1, D]])

    def yout_tile_ap(lt):
        return bass.AP(yout, lt * LTILE * D, [[D, 128], [128 * D, NSUB], [1, D]])

    def row128_ap(h, i):
        return bass.AP(h, 128 * i * D, [[D, 128], [1, D]])

    es = ExitStack()
    sb = lambda nm, shp, dt=F32: es.enter_context(nc.sbuf_tensor(nm, shp, dt))
    mkpsum = lambda nm, shp, dt=F32: es.enter_context(nc.psum_tensor(nm, shp, dt))

    ident = sb("ident_t", [128, 128], BF16)
    wq = sb("wq_t", [128, NIC, D], BF16)
    wo = sb("wo_t", [128, NIC, D], BF16)
    wdT = sb("wdT_t", [128, NIC, D], BF16)
    knT = sb("knT_t", [128, NIC, S], BF16)
    vplus = sb("vplus_t", [128, NCHUNK, H * 65], BF16)
    g_rep = sb("g_rep_t", [128, D], F32)
    b_rep = sb("b_rep_t", [128, D], F32)
    ones_row = sb("ones_row", [1, 128], BF16)
    bq_r = sb("bq_r", [1, D], BF16)
    bk_r = sb("bk_r", [1, D], BF16)
    bv_r = sb("bv_r", [1, D], BF16)
    bo_r = sb("bo_r", [1, D], BF16)
    fb_r = sb("fb_r", [1, D], BF16)
    TL = sb("TL", [128, D], F32)
    T1 = sb("T1", [128, D], F32)
    T2 = sb("T2", [128, D], F32)
    I1 = sb("I1", [128, D], I32)
    I2 = sb("I2", [128, D], I32)
    AMX = sb("AMX", [128, 32], F32)
    SCL = sb("SCL", [128, 32], F32)
    RS8 = sb("RS8", [128, 32], F32)
    SC8 = sb("SC8", [128, 32], F32)
    stat6 = sb("stat6", [128, 12], F32)
    mv = sb("mv", [128, 2], F32)
    sd1 = sb("sd1", [128, 1], F32)
    rstd = sb("rstd", [128, 1], F32)
    nq16 = sb("nq16", [128, H], F32)
    rq16 = sb("rq16", [128, H], F32)
    xd_bf = sb("xd_bf", [128, D], BF16)
    ln_bf = sb("ln_bf", [128, D], BF16)
    qn_bf = sb("qn_bf", [128, D], BF16)
    stage_bf = ln_bf[64:128, 0:512]

    psA = mkpsum("psA", [128, 1024], F32)
    psB = mkpsum("psB", [128, 1024], F32)
    ctxps = mkpsum("ctxps", [128, 1024], F32)
    trpsA = mkpsum("trpsA", [128, 1024], BF16)
    trpsB = mkpsum("trpsB", [128, 1024], BF16)
    psname = {id(psA): "psA", id(psB): "psB", id(ctxps): "ctxps"}

    p = Plan(nc)
    LD = Ctr(es.enter_context(nc.semaphore("LD")), "LD")
    ST = Ctr(es.enter_context(nc.semaphore("ST")), "ST")
    TS = Ctr(es.enter_context(nc.semaphore("TS")), "TS")
    VS = Ctr(es.enter_context(nc.semaphore("VS")), "VS")
    SC = Ctr(es.enter_context(nc.semaphore("SC")), "SC")
    GP = Ctr(es.enter_context(nc.semaphore("GP")), "GP")
    p.hs_ctr = VS

    free = {}

    def wait_free(eng, key):
        if key in free and free[key] is not None:
            ctr, val = free[key]
            p.wait(eng, ctr, val)

    def dma_in(out_ap, in_ap):
        return p.emit("sync", lambda e, o=out_ap, i=in_ap:
                      e.dma_start(out=o, in_=i), inc=LD, k=16)

    def bcast0(t, nalloc, n, inner):
        return bass.AP(t, 0, [[nalloc, 128], [1, n], [0, inner]])

    tr_par = [0]

    def trans_seq(src_fn, dst3_fn, n, after=None):
        par = tr_par[0]
        tr_par[0] ^= 1
        trp = trpsA if par == 0 else trpsB
        if after is not None:
            if isinstance(after, tuple):
                p.wait("tensor", after[0], after[1])
            else:
                p.wait("tensor", VS, after)
        wait_free("tensor", ("trps", par))
        t_tgt = None
        for j in range(n):
            t_tgt = p.emit("tensor", lambda e, s=src_fn(j), o=trp, j=j:
                           e.transpose(o[:, 128 * j:128 * j + 128], s,
                                       ident[:]), inc=TS)
        p.wait("scalar", TS, t_tgt)
        s_tgt = p.emit("scalar", lambda e, d=dst3_fn(), t=trp, n=n: e.copy(
            d, t[:, 0:128 * n].rearrange("p (a b) -> p a b", b=128)), inc=SC)
        free[("trps", par)] = (SC, s_tgt)
        return s_tgt

    def layernorm(src_ap, eps, out_ap):
        p.emit("vector", lambda e, s=src_ap: e.bn_stats(stat6[:, 0:6], s[:, 0:512]), hs=True)
        p.emit("vector", lambda e, s=src_ap: e.bn_stats(stat6[:, 6:12], s[:, 512:1024]), hs=True)
        p.emit("vector", lambda e: e.bn_aggr(mv[:, :], stat6[:, :]), hs=True)
        v1 = p.emit("vector", lambda e, ep=eps: e.tensor_scalar(
            sd1[:, :], mv[:, 1:2], ep, None, ALU.add), inc=VS)
        p.wait("scalar", VS, v1)
        p.emit("scalar", lambda e: e.activation(rstd[:, :], sd1[:, :], ACT.Ln))
        s1 = p.emit("scalar", lambda e: e.activation(
            sd1[:, :], rstd[:, :], ACT.Exp, bias=0.0, scale=-0.5), inc=SC)
        p.wait("vector", SC, s1)
        return p.emit("vector", lambda e, s=src_ap, o=out_ap: e.tensor_scalar(
            o, s, mv[:, 0:1], sd1[:, :], ALU.subtract, ALU.mult), inc=VS, hs=True)

    def quantize(src_t, out_bf_ap3):
        src3 = src_t[:].rearrange("p (a b) -> p a b", b=32)
        p.emit("vector", lambda e, s=src3: e.tensor_reduce(
            AMX[:, :], s, AX.X, ALU.max, apply_absolute_value=True), hs=True)
        p.emit("vector", lambda e: e.tensor_scalar(
            SCL[:, :], AMX[:, :], 1.0 / 7.5, 1e-30, ALU.mult, ALU.max), hs=True)
        p.emit("vector", lambda e: e.reciprocal(AMX[:, :], SCL[:, :]), hs=True)
        p.emit("vector", lambda e: e.tensor_scalar(
            RS8[:, :], AMX[:, :], 8.0, None, ALU.mult), hs=True)
        p.emit("vector", lambda e: e.tensor_scalar(
            SC8[:, :], SCL[:, :], 0.125, None, ALU.mult), hs=True)
        t1_3 = T1[:].rearrange("p (a b) -> p a b", b=32)
        p.emit("vector", lambda e, s=src3: e.tensor_tensor(
            t1_3, bcast0(RS8, 32, 32, 32), s, ALU.mult))
        p.emit("vector", lambda e: e.tensor_scalar(
            I1[:, :], T1[:, :], 16.0, -16.0, ALU.min, ALU.max))
        p.emit("vector", lambda e: e.tensor_scalar(
            T2[:, :], T1[:, :], 0.5, None, ALU.mult))
        p.emit("vector", lambda e: e.tensor_scalar(
            T1[:, :], T2[:, :], 16.0, -16.0, ALU.min, ALU.max))
        p.emit("vector", lambda e: e.tensor_scalar(
            TL[:, :], T2[:, :], 8.0, -8.0, ALU.min, ALU.max))
        p.emit("vector", lambda e: e.tensor_tensor(
            I2[:, :], T1[:, :], TL[:, :], ALU.subtract))
        p.emit("vector", lambda e: e.tensor_scalar(
            T1[:, :], T2[:, :], 0.5, None, ALU.mult))
        p.emit("vector", lambda e: e.tensor_scalar(
            TL[:, :], T1[:, :], 8.0, -8.0, ALU.min, ALU.max))
        p.emit("vector", lambda e: e.scalar_tensor_tensor(
            T2[:, :], I2[:, :], 2.0, I1[:, :], ALU.mult, ALU.add), hs=True)
        p.emit("vector", lambda e: e.tensor_tensor(
            I1[:, :], T1[:, :], TL[:, :], ALU.subtract))
        p.emit("vector", lambda e: e.scalar_tensor_tensor(
            T1[:, :], I1[:, :], 4.0, T2[:, :], ALU.mult, ALU.add), hs=True)
        return p.emit("vector", lambda e, o=out_bf_ap3: e.tensor_tensor(
            o, bcast0(SC8, 32, 32, 32), t1_3, ALU.mult), inc=VS)

    def proj_matmuls(ps, lhsT_fn, rhs_t, bias_r, operands_ready):
        wait_free("tensor", psname[id(ps)])
        if isinstance(operands_ready, tuple):
            p.wait("tensor", operands_ready[0], operands_ready[1])
        else:
            p.wait("tensor", VS, operands_ready)
        tgt = None
        for half in range(2):
            lo = 512 * half
            for j in range(NIC):
                p.emit("tensor", lambda e, ps=ps, lo=lo, l=lhsT_fn(j),
                       r=rhs_t[:, j, lo:lo + 512], st=(j == 0):
                       e.matmul(ps[:, lo:lo + 512], l, r, start=st, stop=False))
            tgt = p.emit("tensor", lambda e, ps=ps, lo=lo, b=bias_r:
                         e.matmul(ps[:, lo:lo + 512], ones_row[0:1, :],
                                  b[:, lo:lo + 512], start=False, stop=True),
                         inc=TS)
        return tgt

    id_ld = dma_in(ident[:], id_h.ap())
    dma_in(g_rep[:], g_h.ap())
    dma_in(b_rep[:], b_h.ap())
    p.wait("tensor", LD, id_ld)

    for hsrc, dstr in [(bq_h, bq_r), (bk_h, bk_r), (bv_h, bv_r),
                       (bo_h, bo_r), (fb_h, fb_r)]:
        p.emit("gpsimd", lambda e, i=hsrc.ap(), o=dstr[:, :]:
               e.dma_start(out=o, in_=i), inc=ST, k=16)
    bias_ld = ST.n
    p.wait("tensor", ST, bias_ld)

    p.emit("vector", lambda e: e.memset(ones_row[:, :], 1.0))
    ones_v = p.emit("vector", lambda e: e.memset(
        bass.AP(vplus, 64, [[NCHUNK * H * 65, 128], [H * 65, NCHUNK], [65, H]]),
        1.0), inc=VS)

    xdT = sb("xdT", [128, NIC, LTILE], BF16)
    lnT = sb("lnT", [128, NIC, LTILE], BF16)
    qnT = lnT
    ctx_bf = xdT
    PT = sb("PT", [128, 1, 1024], BF16)

    pre = ExitStack()
    wk = pre.enter_context(nc.sbuf_tensor("wk_t", [128, NIC, D], BF16))
    wv = pre.enter_context(nc.sbuf_tensor("wv_t", [128, NIC, D], BF16))
    wdq_c = PT[:, 0, :]
    srcT = ln_bf[:].rearrange("p (a b) -> p a b", b=128)
    src_bf = qn_bf
    kn_bf = xd_bf
    valT = kn_bf[:].rearrange("p (a b) -> p a b", b=128)
    valT2 = valT

    for wi, (h, dst) in enumerate([(wq_h, wq), (wk_h, wk), (wv_h, wv),
                                   (wo_h, wo)]):
        for i in range(NIC):
            p.emit("gpsimd", lambda e, i_=row128_ap(h, i), o=dst[:, i, :]:
                   e.dma_start(out=o, in_=i_), inc=ST, k=16)
    w_ld = ST.n
    for oc in range(NIC):
        ld = dma_in(TL[:, :], row128_ap(wd_h, oc))
        p.wait("vector", LD, ld)
        qv = quantize(TL, wdq_c.rearrange("p (a b) -> p a b", b=32))
        trans_seq(lambda j: wdq_c[:, 128 * j:128 * j + 128],
                  lambda oc=oc: bass.AP(wdT, 128 * oc,
                                        [[NIC * D, 128], [D, NIC], [1, 128]]),
                  NIC, after=qv)
        free["wdld"] = (VS, qv)
        wait_free("sync", "wdld")

    def elu_norm(src_ps, out_bf):
        p.emit("vector", lambda e, ps=src_ps: e.tensor_scalar(
            T1[:, :], ps[:, 0:1024], 0.0, None, ALU.min))
        e2 = p.emit("vector", lambda e, ps=src_ps: e.tensor_scalar(
            T2[:, :], ps[:, 0:1024], 0.0, None, ALU.max), inc=VS, hs=True)
        p.wait("scalar", VS, e2)
        se = p.emit("scalar", lambda e: e.activation(TL[:, :], T1[:, :],
                                                     ACT.Exp), inc=SC)
        p.wait("vector", SC, se)
        p.emit("vector", lambda e: e.scalar_tensor_tensor(
            T1[:, :], TL[:, :], -1.0, T2[:, :], ALU.add, ALU.add), hs=True)
        p.emit("vector", lambda e: e.tensor_tensor(
            T2[:, :], T1[:, :], T1[:, :], ALU.mult))
        p.emit("vector", lambda e: e.tensor_reduce(
            nq16[:, :], T2[:].rearrange("p (h e) -> p h e", e=E), AX.X,
            ALU.add), hs=True)
        vr = VS.n
        p.wait("scalar", VS, vr)
        p.emit("scalar", lambda e: e.activation(rq16[:, :], nq16[:, :], ACT.Ln))
        sr = p.emit("scalar", lambda e: e.activation(
            nq16[:, :], rq16[:, :], ACT.Exp, bias=0.0, scale=-0.5), inc=SC)
        p.wait("vector", SC, sr)
        return p.emit("vector", lambda e, o=out_bf: e.tensor_tensor(
            o.rearrange("p (h e) -> p h e", e=E), bcast0(nq16, H, H, E),
            T1[:].rearrange("p (h e) -> p h e", e=E), ALU.mult), inc=VS), e2

    srcT_b = [lnT[:, :, 0:128], lnT[:, :, 128:256]]
    valT_s = lnT[:, :, 256:384]
    kn_b = xd_bf
    tails = []

    def kv_tail():
        st, kt, ldv, kps = tails.pop(0)
        p.wait("vector", TS, kt)
        wait_free("vector", "knb")
        ek, kps_free = elu_norm(kps, kn_b)
        free[psname[id(kps)]] = (VS, kps_free)
        knd = trans_seq(lambda j: kn_b[:, 128 * j:128 * j + 128],
                        lambda st=st: bass.AP(knT, 128 * st,
                                              [[NIC * S, 128], [S, NIC],
                                               [1, 128]]),
                        NIC, after=ek)
        free["knb"] = (SC, knd)
        b = st % 2
        wait_free("tensor", "valT")
        rdyv = trans_seq(lambda j, b=b: (PT[:, 0, 128 * j:128 * j + 128]
                                         if b == 0 else lnT[:, j, 384:512]),
                         lambda: valT_s[:, :, :], NIC, after=(ST, ldv))
        free[("kvval", b)] = (SC, rdyv)
        vt = proj_matmuls(psB, lambda j: valT_s[:, j, :], wv, bv_r, (SC, rdyv))
        free["valT"] = (TS, vt)
        p.wait("vector", TS, vt)
        vv = p.emit("vector", lambda e, st=st: e.tensor_copy(
            bass.AP(vplus, st * H * 65,
                    [[NCHUNK * H * 65, 128], [65, H], [1, E]]),
            psB[:].rearrange("p (h e) -> p h e", e=E)), inc=VS)
        free["psB"] = (VS, vv)

    p.wait("tensor", ST, w_ld)
    free[("kvval", 0)] = (SC, SC.n)
    for st in range(NCHUNK):
        b = st % 2
        wait_free("gpsimd", ("kvsrc", b))
        lds = p.emit("gpsimd", lambda e, i_=row128_ap(src_h, st), b=b:
                     e.dma_start(out=(qn_bf if b == 0 else ln_bf)[:, :],
                                 in_=i_), inc=ST, k=16)
        wait_free("gpsimd", ("kvval", b))
        ldv = p.emit("gpsimd", lambda e, i_=row128_ap(val_h, st), b=b:
                     e.dma_start(out=(PT[:, 0, :] if b == 0
                                      else bass.AP(lnT, 384,
                                                   [[NIC * 512, 128],
                                                    [512, NIC], [1, 128]])),
                                 in_=i_), inc=ST, k=16)
        wait_free("tensor", ("srcT", b))
        rdy = trans_seq(lambda j, b=b: (qn_bf if b == 0 else ln_bf)[:, 128 * j:128 * j + 128],
                        lambda b=b: srcT_b[b][:, :, :], NIC, after=(ST, lds))
        free[("kvsrc", b)] = (SC, rdy)
        kps = psA if b == 0 else ctxps
        kt = proj_matmuls(kps, lambda j, b=b: srcT_b[b][:, j, :], wk, bk_r,
                          (SC, rdy))
        free[("srcT", b)] = (TS, kt)
        tails.append((st, kt, ldv, kps))
        if len(tails) >= 2:
            kv_tail()
    while tails:
        kv_tail()

    pre_done_v, pre_done_t = VS.n, TS.n
    pre.close()

    xA = sb("xA", [128, NSUB, D], F32)
    xB = sb("xB", [128, NSUB, D], F32)

    for e in ENGINES:
        p.wait(e, VS, pre_done_v)
        p.wait(e, TS, pre_done_t)

    dbg_done = [False]

    def ffn_phase(xsrc, xdst):
        pend = []

        def flush_epilogue():
            ps, s, mm = pend.pop(0)
            p.wait("vector", TS, mm)
            wait_free("vector", ("x", id(xdst)))
            ev = p.emit("vector", lambda e, ps=ps, s=s, xs=xsrc, xd=xdst:
                        e.scalar_tensor_tensor(xd[:, s, :], ps[:, 0:1024], 0.5,
                                               xs[:, s, :], ALU.mult, ALU.add),
                        inc=VS)
            free[psname[id(ps)]] = (VS, ev)
        for s in range(NSUB):
            layernorm(xsrc[:, s, :], 1e-5, TL[:, :])
            qt = quantize(TL, xd_bf[:].rearrange("p (a b) -> p a b", b=32))
            if dbg and not dbg_done[0]:
                p.wait("gpsimd", VS, qt)
                p.emit("gpsimd", lambda e: e.dma_start(out=dbg_ln, in_=TL[:, :]),
                       inc=ST, k=16)
                p.wait("gpsimd", VS, qt)
                p.emit("gpsimd", lambda e: e.dma_start(out=dbg_xd, in_=xd_bf[:, :]),
                       inc=ST, k=16)
                dbg_st = ST.n
                free[("dbghold")] = (ST, dbg_st)
            rdy = trans_seq(lambda j: xd_bf[:, 128 * j:128 * j + 128],
                            lambda s=s: bass.AP(xdT, 128 * s,
                                                [[NIC * LTILE, 128],
                                                 [LTILE, NIC], [1, 128]]),
                            NIC, after=qt)
            if dbg and not dbg_done[0]:
                p.wait("gpsimd", SC, rdy)
                p.emit("gpsimd", lambda e: e.dma_start(
                    out=dbg_xdT, in_=xdT[:, :, 0:128]), inc=ST, k=16)
                p.wait("vector", ST, ST.n)
                dbg_done[0] = True
            ps = psA if s % 2 == 0 else psB
            mm = proj_matmuls(ps, lambda j, s=s: xdT[:, j, 128 * s:128 * s + 128],
                              wdT, fb_r, (SC, rdy))
            pend.append((ps, s, mm))
            if len(pend) >= 2:
                flush_epilogue()
        while pend:
            flush_epilogue()
        return

    dbg2_done = [False]

    def q_phase(x1):
        pend = []

        def flush_elu():
            ps, s, mm = pend.pop(0)
            p.wait("vector", TS, mm)
            wait_free("vector", "qn_bf")
            eq, ps_free = elu_norm(ps, qn_bf)
            free[psname[id(ps)]] = (VS, ps_free)
            qdone = trans_seq(lambda j: qn_bf[:, 128 * j:128 * j + 128],
                              lambda s=s: bass.AP(qnT, 128 * s,
                                                  [[NIC * LTILE, 128],
                                                   [LTILE, NIC], [1, 128]]),
                              NIC, after=eq)
            free["qn_bf"] = (SC, qdone)
        for s in range(NSUB):
            layernorm(x1[:, s, :], 1e-5, TL[:, :])
            cb = p.emit("vector", lambda e: e.tensor_copy(ln_bf[:, :], TL[:, :]))
            rdy = trans_seq(lambda j: ln_bf[:, 128 * j:128 * j + 128],
                            lambda s=s: bass.AP(lnT, 128 * s,
                                                [[NIC * LTILE, 128],
                                                 [LTILE, NIC], [1, 128]]),
                            NIC, after=cb)
            ps = psA if s % 2 == 0 else psB
            mm = proj_matmuls(ps, lambda j, s=s: lnT[:, j, 128 * s:128 * s + 128],
                              wq, bq_r, (SC, rdy))
            pend.append((ps, s, mm))
            if len(pend) >= 2:
                flush_elu()
        while pend:
            flush_elu()
        return SC.n

    def attn_phase(x1, x2, qn_done):
        st_dma = []
        p.wait("tensor", SC, qn_done)
        for pr in range(H // 2):
            h0, h1 = 2 * pr, 2 * pr + 1
            cmm = None
            ex_t = {}
            for c in range(NCHUNK + 1):
                if c < NCHUNK:
                    ps = psA if c % 2 == 0 else psB
                    ptb = PT[:, 0, :] if c % 2 == 0 else qn_bf[:, 0:1024]
                    wait_free("tensor", psname[id(ps)])
                    p.emit("tensor", lambda e, ps=ps, pr=pr, c=c: e.matmul(
                        ps[:, 0:512], knT[0:64, pr, 128 * c:128 * c + 128],
                        qnT[0:64, pr, :], start=True, stop=True))
                    smm = p.emit("tensor", lambda e, ps=ps, pr=pr, c=c: e.matmul(
                        ps[:, 512:1024], knT[64:128, pr, 128 * c:128 * c + 128],
                        qnT[64:128, pr, :], start=True, stop=True,
                        tile_position=(64, 0)), inc=TS)
                    p.wait("scalar", TS, smm)
                    wait_free("scalar", ("PT", c % 2))
                    ex = p.emit("scalar", lambda e, o=ptb, ps=ps: e.activation(
                        o, ps[:, 0:1024], ACT.Exp, bias=0.0,
                        scale=ATT_SCALE), inc=SC)
                    free[psname[id(ps)]] = (SC, ex)
                    ex_t[c] = ex
                if c >= 1:
                    cc = c - 1
                    pcb = PT[:, 0, :] if cc % 2 == 0 else qn_bf[:, 0:1024]
                    p.wait("tensor", SC, ex_t[cc])
                    if cc == 0:
                        wait_free("tensor", "ctxps")
                    p.emit("tensor", lambda e, cc=cc, h0=h0, o=pcb: e.matmul(
                        ctxps[0:65, 0:512], vplus[:, cc, 65 * h0:65 * h0 + 65],
                        o[:, 0:512], start=(cc == 0), stop=(cc == NCHUNK - 1)))
                    cmm = p.emit("tensor", lambda e, cc=cc, h1=h1, o=pcb: e.matmul(
                        ctxps[0:65, 512:1024],
                        vplus[:, cc, 65 * h1:65 * h1 + 65],
                        o[:, 512:1024], start=(cc == 0),
                        stop=(cc == NCHUNK - 1)), inc=TS)
                    free[("PT", cc % 2)] = (TS, cmm)
            p.wait("vector", TS, cmm)
            p.wait("scalar", TS, cmm)
            p.emit("scalar", lambda e: e.activation(
                TL[0:1, 0:1024], ctxps[64:65, 0:1024], ACT.Ln))
            rb = p.emit("scalar", lambda e: e.activation(
                ln_bf[0:1, 0:1024], TL[0:1, 0:1024], ACT.Exp, bias=0.0,
                scale=-1.0), inc=SC)
            ct = p.emit("vector", lambda e: e.tensor_copy(
                T1[0:64, 0:1024], ctxps[0:64, 0:1024]), inc=VS)
            free["ctxps"] = (VS, ct)
            if dbg2 and not dbg2_done[0]:
                p.wait("gpsimd", VS, ct)
                p.emit("gpsimd", lambda e: e.dma_start(out=dbg_ctx,
                                                       in_=T1[0:64, 0:1024]),
                       inc=ST, k=16)
                p.emit("gpsimd", lambda e: e.dma_start(out=dbg_rs,
                                                       in_=TL[0:1, 0:1024]),
                       inc=ST, k=16)
                p.wait("vector", ST, ST.n)
                dbg2_done[0] = True
            p.wait("tensor", SC, rb)
            p.wait("tensor", VS, ct)
            wait_free("tensor", "psA")
            p.emit("tensor", lambda e: e.matmul(
                psA[0:64, 0:512], ones_row[0:1, 0:64], ln_bf[0:1, 0:512],
                start=True, stop=True))
            bc = p.emit("tensor", lambda e: e.matmul(
                psA[0:64, 512:1024], ones_row[0:1, 0:64], ln_bf[0:1, 512:1024],
                start=True, stop=True), inc=TS)
            p.wait("vector", TS, bc)
            p.emit("vector", lambda e, pr=pr: e.tensor_tensor(
                ctx_bf[0:64, pr, :], T1[0:64, 0:512], psA[0:64, 0:512],
                ALU.mult))
            wait_free("vector", "stage_odd")
            nv = p.emit("vector", lambda e: e.tensor_tensor(
                stage_bf, T1[0:64, 512:1024], psA[0:64, 512:1024],
                ALU.mult), inc=VS)
            free["psA"] = (VS, nv)
            p.wait("gpsimd", VS, nv)
            sdm = p.emit("gpsimd", lambda e, pr=pr: e.dma_start(
                out=ctx_bf[64:128, pr, :], in_=stage_bf), inc=ST, k=16)
            free["stage_odd"] = (ST, sdm)
            st_dma.append(sdm)
        ctx_done = VS.n
        p.wait("tensor", ST, st_dma[-1])
        pend = []

        def flush_out():
            ps, s, mm = pend.pop(0)
            p.wait("vector", TS, mm)
            layernorm(x1[:, s, :], 1e-5, TL[:, :])
            ev1 = p.emit("vector", lambda e, ps=ps: e.tensor_tensor(
                T1[:, :], ps[:, 0:1024], TL[:, :], ALU.add), inc=VS)
            free[psname[id(ps)]] = (VS, ev1)
            layernorm(T1[:, :], 1e-12, T2[:, :])
            p.emit("vector", lambda e: e.tensor_tensor(
                TL[:, :], T2[:, :], g_rep[:, :], ALU.mult))
            p.emit("vector", lambda e: e.tensor_tensor(
                T2[:, :], TL[:, :], b_rep[:, :], ALU.add))
            wait_free("vector", ("x", id(x2)))
            p.emit("vector", lambda e, s=s, x1=x1, x2=x2: e.tensor_tensor(
                x2[:, s, :], x1[:, s, :], T2[:, :], ALU.add), inc=VS)
        for s in range(NSUB):
            ps = psA if s % 2 == 0 else psB
            mm = proj_matmuls(ps, lambda j, s=s:
                              ctx_bf[:, j, 128 * s:128 * s + 128],
                              wo, bo_r, ctx_done)
            pend.append((ps, s, mm))
            if len(pend) >= 2:
                flush_out()
        while pend:
            flush_out()

    for lt in range(NT):
        wait_free("sync", ("x", id(xA)))
        ldx = dma_in(xA[:], xin_tile_ap(lt))
        p.wait("vector", LD, ldx)
        ffn_phase(xA, xB)
        if stage >= 2:
            qn_done = q_phase(xB)
            attn_phase(xB, xA, qn_done)
            if stage >= 3:
                ffn_phase(xA, xB)
        out_buf = xA if stage == 2 else xB
        p.wait("gpsimd", VS, VS.n)
        stt = p.emit("gpsimd", lambda e, lt=lt, ob=out_buf: e.dma_start(
            out=yout_tile_ap(lt), in_=ob[:]), inc=ST, k=16)
        free[("x", id(xA))] = (ST, stt)
        free[("x", id(xB))] = (ST, stt)

    p.wait("gpsimd", ST, ST.n)

    p.check_deadlock()
    with nc.Block() as block:
        @block.sync
        def _(e):
            for fn in p.ops["sync"]:
                fn(e)

        @block.gpsimd
        def _(e):
            for fn in p.ops["gpsimd"]:
                fn(e)

        @block.tensor
        def _(e):
            for fn in p.ops["tensor"]:
                fn(e)

        @block.vector
        def _(e):
            for fn in p.ops["vector"]:
                fn(e)

        @block.scalar
        def _(e):
            for fn in p.ops["scalar"]:
                fn(e)

    es.close()
    return nc


def kernel(L0_attn_target_embedding, L0_attn_source_embedding,
           L0_attn_value_embedding, L0_attn_Wq, L0_attn_bq, L0_attn_Wk,
           L0_attn_bk, L0_attn_Wv, L0_attn_bv, L0_attn_Wo, L0_attn_bo,
           L0_attn_ln_g, L0_attn_ln_b, L0_ffn_weight, L0_ffn_bias):
    x = np.ascontiguousarray(np.asarray(L0_attn_target_embedding, dtype=np.float32))
    B, L, Dm = x.shape
    shared = {
        "src": np.ascontiguousarray(np.asarray(L0_attn_source_embedding, np.float32)),
        "val": np.ascontiguousarray(np.asarray(L0_attn_value_embedding, np.float32)),
        "wq": np.ascontiguousarray(np.asarray(L0_attn_Wq, np.float32)),
        "wk": np.ascontiguousarray(np.asarray(L0_attn_Wk, np.float32)),
        "wv": np.ascontiguousarray(np.asarray(L0_attn_Wv, np.float32)),
        "wo": np.ascontiguousarray(np.asarray(L0_attn_Wo, np.float32)),
        "wd": np.ascontiguousarray(np.asarray(L0_ffn_weight, np.float32)),
        "bq": np.ascontiguousarray(np.asarray(L0_attn_bq, np.float32).reshape(1, -1)),
        "bk": np.ascontiguousarray(np.asarray(L0_attn_bk, np.float32).reshape(1, -1)),
        "bv": np.ascontiguousarray(np.asarray(L0_attn_bv, np.float32).reshape(1, -1)),
        "bo": np.ascontiguousarray(np.asarray(L0_attn_bo, np.float32).reshape(1, -1)),
        "fb": np.ascontiguousarray(np.asarray(L0_ffn_bias, np.float32).reshape(1, -1)),
        "g_rep": np.ascontiguousarray(np.broadcast_to(
            np.asarray(L0_attn_ln_g, np.float32), (128, Dm))),
        "b_rep": np.ascontiguousarray(np.broadcast_to(
            np.asarray(L0_attn_ln_b, np.float32), (128, Dm))),
        "ident": np.eye(128, dtype=ml_dtypes.bfloat16),
    }
    nc = build_nc(stage=3)
    in_maps = []
    for c in range(NCORE):
        m = dict(shared)
        m["xin"] = np.ascontiguousarray(x[2 * c:2 * c + 2].reshape(LCORE, Dm))
        in_maps.append(m)
    res = run_bass_kernel_spmd(nc, in_maps, core_ids=list(range(NCORE)))
    out = np.empty((B, L, Dm), dtype=np.float32)
    for c in range(NCORE):
        out[2 * c:2 * c + 2] = res.results[c]["yout"].reshape(2, L, Dm)
    return out
